# version 26
# baseline (speedup 1.0000x reference)
import numpy as np
import concourse.bass as bass
import concourse.mybir as mybir
from concourse.bass_utils import run_bass_kernel_spmd
from contextlib import ExitStack

F32 = mybir.dt.float32
BF16 = mybir.dt.bfloat16
AF = mybir.ActivationFunctionType
ALU = mybir.AluOpType
AX = mybir.AxisListType

T, D, H, HS, HP, DC, NTT, NTC = 4096, 1024, 16, 64, 8, 8, 32, 8
NE, IN = 16, 512

ENGINES = ("sync", "scalar", "vector", "tensor", "gpsimd")
NSLOT = {"sync": 12, "gpsimd": 8}

DBG = False


class Em:

    def __init__(self):
        self.tapes = {e: [] for e in ENGINES}
        self.ecnt = {e: 0 for e in ENGINES}
        self.slot_uses = {q: [0] * NSLOT[q] for q in NSLOT}
        self.dcnt = {q: 0 for q in NSLOT}
        self.know = {e: {} for e in ENGINES}
        self.snap = {}
        self.bufs = {}
        self.legacy = {}
        self.sems = {}

    def _buf(self, b):
        s = self.bufs.get(b)
        if s is None:
            s = {"w": None, "r": [tuple(kv) for kv in self.legacy.items()]}
            self.bufs[b] = s
        return s

    def _needs(self, eng, reads, writes):
        need = {}
        k = self.know[eng]

        def add(tok):
            if tok is None:
                return
            s, v = tok
            if eng == "tensor" and s == "e_tensor":
                return
            if k.get(s, 0) >= v:
                return
            if need.get(s, 0) < v:
                need[s] = v

        for b in reads:
            add(self._buf(b)["w"])
        for b in writes:
            st = self._buf(b)
            add(st["w"])
            for t in st["r"]:
                add(t)
        return need

    def _absorb(self, eng, need):
        k = self.know[eng]
        for s, v in need.items():
            sn = self.snap.get((s, v))
            if sn:
                for s2, v2 in sn.items():
                    if k.get(s2, 0) < v2:
                        k[s2] = v2
            if k.get(s, 0) < v:
                k[s] = v

    def _record(self, tok, eng, reads, writes):
        sn = dict(self.know[eng])
        sn[tok[0]] = max(sn.get(tok[0], 0), tok[1])
        self.snap[tok] = sn
        for b in reads:
            self._buf(b)["r"].append(tok)
        for b in writes:
            st = self._buf(b)
            st["w"] = tok
            st["r"] = []

    def emit(self, eng, fn, reads=(), writes=()):
        need = self._needs(eng, reads, writes)
        self._absorb(eng, need)
        self.ecnt[eng] += 1
        tok = ("e_" + eng, self.ecnt[eng])
        if eng == "tensor":
            self.know[eng]["e_tensor"] = self.ecnt["tensor"]
        self._record(tok, eng, reads, writes)
        waits = sorted(need.items())
        sems = self.sems

        def step(E, waits=waits, fn=fn, tok=tok):
            for s, v in waits:
                E.wait_ge(sems[s], v)
            fn(E).then_inc(sems[tok[0]], 1)

        self.tapes[eng].append(step)
        return tok

    def dma(self, q, fn, reads=(), writes=()):
        need = self._needs(q, reads, writes)
        self._absorb(q, need)
        slot = self.dcnt[q] % NSLOT[q]
        self.dcnt[q] += 1
        self.slot_uses[q][slot] += 1
        tok = (f"d_{q}{slot}", 16 * self.slot_uses[q][slot])
        self._record(tok, q, reads, writes)
        waits = sorted(need.items())
        sems = self.sems

        def step(E, waits=waits, fn=fn, tok=tok):
            for s, v in waits:
                E.wait_ge(sems[s], v)
            fn(E).then_inc(sems[tok[0]], 16)

        self.tapes[q].append(step)
        return tok

    def retire(self, buf_ids):
        for b in buf_ids:
            st = self.bufs.pop(b, None)
            if st is None:
                continue
            toks = list(st["r"])
            if st["w"] is not None:
                toks.append(st["w"])
            for s, v in toks:
                if self.legacy.get(s, 0) < v:
                    self.legacy[s] = v

    def retire_prefix(self, prefix):
        ids = [b for b in self.bufs
               if isinstance(b, tuple) and len(b) > 0 and b[0] == prefix]
        self.retire(ids)

    def final_waits(self):
        fin = {}
        for e in ENGINES:
            if self.ecnt[e]:
                fin["e_" + e] = self.ecnt[e]
        for q in NSLOT:
            for sl, u in enumerate(self.slot_uses[q]):
                if u:
                    fin[f"d_{q}{sl}"] = 16 * u
        sems = self.sems

        def step(E, fin=sorted(fin.items())):
            for s, v in fin:
                E.wait_ge(sems[s], v)

        self.tapes["sync"].append(step)

    def run_block(self, nc):
        with nc.Block() as blk:
            for ename in ENGINES:
                steps = self.tapes[ename]
                if steps:
                    getattr(blk, ename)(
                        lambda eng, steps=steps: [f(eng) for f in steps])
        self.tapes = {e: [] for e in ENGINES}


def build():
    nc = bass.Bass("TRN2", target_bir_lowering=False, debug=False)
    em = Em()

    dr = lambda n, s, d, k: nc.dram_tensor(n, s, d, kind=k).ap()
    x_d = dr("x", [T, D], F32, "ExternalInput")
    wq_d = dr("wq", [D, D], F32, "ExternalInput")
    wk_d = dr("wk", [D, D], F32, "ExternalInput")
    wv_d = dr("wv", [D, D], F32, "ExternalInput")
    wo_d = dr("wo", [D, D], F32, "ExternalInput")
    gw_d = dr("gwt", [D, 16], F32, "ExternalInput")
    we1_d = dr("we1", [NE * D, IN], F32, "ExternalInput")
    we3_d = dr("we3", [NE * D, IN], F32, "ExternalInput")
    we2_d = dr("we2", [NE * IN, D], F32, "ExternalInput")
    ws1_d = dr("ws1", [D, D], F32, "ExternalInput")
    ws3_d = dr("ws3", [D, D], F32, "ExternalInput")
    ws2_d = dr("ws2", [D, D], F32, "ExternalInput")
    be2_d = dr("be2", [NE, D], F32, "ExternalInput")
    bc_d = dr("biascols", [128, 160], F32, "ExternalInput")
    rv_d = dr("rowvec", [1, 3216], F32, "ExternalInput")
    mk_d = dr("masks", [128, 2048], F32, "ExternalInput")
    ey_d = dr("eye", [128, 128], F32, "ExternalInput")
    out_d = dr("out", [T, D], F32, "ExternalOutput")
    scratch_kind = "ExternalOutput" if DBG else "Internal"
    x1fm = dr("x1fm", [D, T], BF16, scratch_kind)
    dbgc = dr("dbgctm", [128, 512], F32, scratch_kind) if DBG else None
    dbgy = dr("dbgy", [256, 16384], F32, scratch_kind) if DBG else None
    xp_d = dr("xp", [T, D], F32, scratch_kind)
    x2fm = dr("x2tm", [T, D], BF16, scratch_kind)
    x2lo = dr("x2lotm", [T, D], BF16, scratch_kind)
    xfmd = dr("xfmd", [D, T], BF16, scratch_kind)

    stack = ExitStack()
    with stack:
        sem_names = ["e_" + e for e in ENGINES]
        for q in NSLOT:
            sem_names += [f"d_{q}{i}" for i in range(NSLOT[q])]
        for s in sem_names:
            em.sems[s] = stack.enter_context(nc.semaphore(s))

        SB = lambda n, sh, dt: stack.enter_context(nc.sbuf_tensor(n, sh, dt))
        bc = SB("bc", [128, 160], F32)
        eyef = SB("eyef", [128, 128], F32)
        eyeb = SB("eyeb", [128, 128], BF16)
        onef = SB("onef", [128, 128], F32)
        cm16 = SB("cm16", [128, 1], F32)
        ceps = SB("ceps", [128, 1], F32)
        gwf = SB("gwf", [128, 128], F32)
        gwhi = SB("gwhi", [128, 128], BF16)
        gwlo = SB("gwlo", [128, 128], BF16)
        sbc = SB("sbc", [128, 1024], F32)
        gbc = SB("gbc", [128, 16], F32)
        ctm = SB("ctm", [128, 512], F32)
        ctT = SB("ctT", [16, 4096], BF16)
        be2b = SB("be2b", [16, 1024], BF16)
        stats = SB("stats", [128, 80], F32)

        g_bc = []

        def man(n, sh, dt):
            g = nc.sbuf_tensor(n, sh, dt)
            t = g.__enter__()
            return g, t

        def manp(n, sh, dt):
            g = nc.psum_tensor(n, sh, dt)
            t = g.__enter__()
            return g, t

        def free(guards):
            for g in reversed(guards):
                g.__exit__(None, None, None)

        gg, msk = man("msk", [128, 2048], F32); g_bc.append(gg)
        gg, rbf = man("rbf", [1, 3216], BF16); g_bc.append(gg)
        gg, obc = man("obc", [128, 1024], F32); g_bc.append(gg)
        gg, cmb = man("cmb", [128, 8 * 4096], BF16); g_bc.append(gg)

        g_rv = []
        gg, rv = man("rv", [1, 3216], F32); g_rv.append(gg)
        g_sp = []
        gg, psu = manp("psu", [128, 1024], F32); g_sp.append(gg)

        E, DMA = em.emit, em.dma
        DMA("sync", lambda s: s.dma_start(bc[:], bc_d), writes=[("bc",)])
        DMA("sync", lambda s: s.dma_start(msk[:], mk_d), writes=[("msk",)])
        DMA("sync", lambda s: s.dma_start(eyef[:], ey_d), writes=[("eyef",)])
        DMA("sync", lambda s: s.dma_start(rv[:], rv_d), writes=[("rv",)])
        DMA("gpsimd", lambda g: g.dma_start(eyeb[:], ey_d), writes=[("eyeb",)])
        DMA("gpsimd", lambda g: g.dma_start(rbf[:], rv_d), writes=[("rbf",)])
        DMA("gpsimd", lambda g: g.dma_start(be2b[:], be2_d), writes=[("be2b",)])
        DMA("gpsimd", lambda g: g.dma_start(
            gwf[:].rearrange("p (c n) -> p c n", c=8),
            gw_d.rearrange("(c p) n -> p c n", p=128)), writes=[("gwf",)])
        E("vector", lambda v: v.memset(onef[:], 1.0), writes=[("onef",)])
        E("vector", lambda v: v.memset(cm16[:], -16.0), writes=[("cm16",)])
        E("vector", lambda v: v.memset(ceps[:], 1e-5), writes=[("ceps",)])
        E("scalar", lambda s: s.activation(gwhi[:], gwf[:], AF.Copy),
          reads=[("gwf",)], writes=[("gwhi",)])
        E("vector", lambda v: v.tensor_sub(gwlo[:], gwf[:], gwhi[:]),
          reads=[("gwf",), ("gwhi",)], writes=[("gwlo",)])
        for name, dst, lo, n in (("obc", obc, 0, 1024), ("sbc", sbc, 1024, 1024),
                                 ("gbc", gbc, 2048, 16)):
            for c0 in range(0, n, 512):
                w = min(512, n - c0)
                E("tensor", lambda t, lo=lo, c0=c0, w=w: t.matmul(
                    psu[:, c0:c0 + w], onef[0:1, 0:128],
                    rv[0:1, lo + c0:lo + c0 + w], start=True, stop=True),
                  reads=[("rv",), ("onef",)], writes=[("psu", c0 // 512)])
            E("vector", lambda v, dst=dst, n=n: v.tensor_copy(dst[:, 0:n], psu[:, 0:n]),
              reads=[("psu", 0), ("psu", 1) if n > 512 else ("psu", 0)],
              writes=[(name,)])
        em.run_block(nc)
        free(g_sp)
        free(g_rv)
        em.retire([("rv",), ("psu", 0), ("psu", 1)])

        g_a = []
        gg, xsb = man("xsb", [128, 3 * 1024], F32); g_a.append(gg)
        gg, x1t = man("x1t", [128, 3 * 1024], BF16); g_a.append(gg)
        gg, stgA = man("stgA", [128, 2 * 4096], BF16); g_a.append(gg)
        gg, ptrA0 = manp("ptrA0", [128, 128], BF16); g_a.append(gg)
        gg, ptrA1 = manp("ptrA1", [128, 128], BF16); g_a.append(gg)
        ptrA = (ptrA0, ptrA1)

        def ln_stats(src_ap, src_ids, sp):
            o = sp * 20
            sid = ("stats", sp)
            E("vector", lambda v: v.bn_stats(stats[:, o:o + 6], src_ap[:, 0:512]),
              reads=src_ids, writes=[sid])
            E("vector", lambda v: v.bn_stats(stats[:, o + 6:o + 12], src_ap[:, 512:1024]),
              reads=src_ids, writes=[sid])
            E("vector", lambda v: v.bn_aggr(stats[:, o + 12:o + 14], stats[:, o:o + 12]),
              reads=[sid], writes=[sid])
            E("vector", lambda v: v.tensor_scalar(stats[:, o + 17:o + 18],
                                                  stats[:, o + 12:o + 13],
                                                  -1.0, None, ALU.mult),
              reads=[sid], writes=[sid])
            E("scalar", lambda s: s.activation(stats[:, o + 14:o + 15],
                                               stats[:, o + 13:o + 14], AF.Ln,
                                               bias=ceps[:], scale=1.0),
              reads=[sid, ("ceps",)], writes=[sid])
            E("scalar", lambda s: s.activation(stats[:, o + 15:o + 16],
                                               stats[:, o + 14:o + 15], AF.Exp,
                                               scale=-0.5),
              reads=[sid], writes=[sid])
            E("scalar", lambda s: s.activation(stats[:, o + 16:o + 17],
                                               stats[:, o + 17:o + 18], AF.Identity,
                                               scale=stats[:, o + 15:o + 16]),
              reads=[sid], writes=[sid])
            return stats[:, o + 15:o + 16], stats[:, o + 16:o + 17], sid

        def a_stage1(tt):
            xb = tt % 3
            sp = tt % 3
            xa = xsb[:, xb * 1024:(xb + 1) * 1024]
            DMA("sync", lambda s, tt=tt, xa=xa: s.dma_start(
                xa, x_d[tt * 128:(tt + 1) * 128, :]), writes=[("xA", xb)])
            ln_stats(xa, [("xA", xb)], sp)

        def a_stage2(tt):
            xb = tt % 3
            sp = tt % 3
            o = sp * 20
            sid = ("stats", sp)
            xa = xsb[:, xb * 1024:(xb + 1) * 1024]
            x1a = x1t[:, sp * 1024:(sp + 1) * 1024]
            E("scalar", lambda s, xa=xa, x1a=x1a, sp=sp, xb=xb, o=o, sid=sid:
              s.activation(x1a, xa, AF.Identity, bias=stats[:, o + 16:o + 17],
                           scale=stats[:, o + 15:o + 16]),
              reads=[("xA", xb), sid], writes=[("x1t", sp)])

        def a_stage3(tt):
            sp = tt % 3
            sgp = (tt // 4) % 2
            x1a = x1t[:, sp * 1024:(sp + 1) * 1024]
            for dc in range(DC):
                pp = dc % 2
                E("tensor", lambda t, x1a=x1a, dc=dc, pp=pp: t.transpose(
                    ptrA[pp][:], x1a[:, dc * 128:(dc + 1) * 128], eyeb[:]),
                  reads=[("x1t", sp), ("eyeb",)], writes=[("ptrA", pp)])
                dco = sgp * 4096 + dc * 512 + (tt % 4) * 128
                if dc % 2 == 0:
                    E("vector", lambda v, pp=pp, dco=dco: v.tensor_copy(
                        stgA[:, dco:dco + 128], ptrA[pp][:]),
                      reads=[("ptrA", pp)], writes=[("stgA", sgp, dc, tt % 4)])
                else:
                    E("scalar", lambda s, pp=pp, dco=dco: s.activation(
                        stgA[:, dco:dco + 128], ptrA[pp][:], AF.Copy),
                      reads=[("ptrA", pp)], writes=[("stgA", sgp, dc, tt % 4)])
            if tt % 4 == 3:
                st = tt // 4
                for dc in range(DC):
                    dco = sgp * 4096 + dc * 512
                    DMA("sync", lambda s, dc=dc, dco=dco, st=st: s.dma_start(
                        x1fm[dc * 128:(dc + 1) * 128, st * 512:(st + 1) * 512],
                        stgA[:, dco:dco + 512]),
                        reads=[("stgA", sgp, dc, j) for j in range(4)],
                        writes=[("x1fm", dc, st)])

        for i in range(NTT + 2):
            if i < NTT:
                a_stage1(i)
            if 1 <= i <= NTT:
                a_stage2(i - 1)
            if 2 <= i:
                a_stage3(i - 2)
        em.run_block(nc)
        free(g_a)
        em.retire_prefix("xA"); em.retire_prefix("x1t"); em.retire_prefix("stgA")
        em.retire_prefix("ptrA")

        g_b = []
        gg, wqs = man("wqs", [128, 2 * 1024], BF16); g_b.append(gg)
        gg, wks = man("wks", [128, 2 * 1024], BF16); g_b.append(gg)
        gg, wvs = man("wvs", [128, 2 * 1024], BF16); g_b.append(gg)
        gg, x1c = man("x1c", [128, 2 * 4096], BF16); g_b.append(gg)
        gg, qsb = man("qsb", [128, 2 * 4096], BF16); g_b.append(gg)
        gg, ksb = man("ksb", [128, 2 * 4096], BF16); g_b.append(gg)
        gg, v65 = man("v65", [128, 2 * 4160], BF16); g_b.append(gg)
        gg, ptb = man("ptb", [128, 4 * 1024], BF16); g_b.append(gg)
        gg, onesbf = man("onesbf", [128, 64], BF16); g_b.append(gg)
        gg, srow = man("srow", [128, 2 * 512], BF16); g_b.append(gg)
        gg, rinv = man("rinv", [64, 2 * 512], F32); g_b.append(gg)
        gg, ost = man("ost", [64, 2 * 512], F32); g_b.append(gg)
        gg, pool0 = manp("pool0", [128, 1024], F32); g_b.append(gg)
        gg, pool1 = manp("pool1", [128, 1024], F32); g_b.append(gg)
        gg, pool2 = manp("pool2", [128, 1024], F32); g_b.append(gg)
        gg, pso0 = manp("pso0", [65, 512], F32); g_b.append(gg)
        gg, pso1 = manp("pso1", [65, 512], F32); g_b.append(gg)
        pool = (pool0, pool1, pool2)
        pso = (pso0, pso1)
        scnt = [0]
        pslots = {}

        def next_slot():
            sl = scnt[0] % 3
            scnt[0] += 1
            return sl

        x1fm_r = x1fm.rearrange("(c p) t -> p c t", p=128)

        E("vector", lambda v: v.memset(onesbf[:], 1.0), writes=[("onesbf",)])
        for par in range(2):
            E("vector", lambda v, par=par: v.memset(
                v65[:, par * 4160 + 64:(par + 1) * 4160:65], 1.0),
              writes=[("v65o", par)])

        def load_qkvw(hp):
            par = hp % 2
            for nm, wsb, wd in (("wq", wqs, wq_d), ("wk", wks, wk_d), ("wv", wvs, wv_d)):
                DMA("gpsimd", lambda g, wsb=wsb, wd=wd, par=par, hp=hp: g.dma_start(
                    wsb[:, par * 1024:(par + 1) * 1024].rearrange("p (c n) -> p c n", c=8),
                    wd.rearrange("(c p) n -> p c n", p=128)[:, :, hp * 128:(hp + 1) * 128]),
                    writes=[(nm, par)])

        def stream_x1(hp, tc):
            xpar = tc % 2
            DMA("sync", lambda s, xpar=xpar, tc=tc: s.dma_start(
                x1c[:, xpar * 4096:(xpar + 1) * 4096].rearrange("p (c t) -> p c t", c=8),
                x1fm_r[:, :, tc * 512:(tc + 1) * 512]),
                reads=[("x1fm", dc, tc) for dc in range(DC)],
                writes=[("x1c", xpar)])
            return xpar

        def emit_V_tt(hp, tc, j):
            par = hp % 2
            xpar = tc % 2
            xco = xpar * 4096
            tt = tc * 4 + j
            sl = next_slot()
            for dc in range(DC):
                E("tensor", lambda t, xco=xco, dc=dc, j=j, par=par, sl=sl: t.matmul(
                    pool[sl][:, 0:128],
                    x1c[:, xco + dc * 512 + j * 128:xco + dc * 512 + (j + 1) * 128],
                    wvs[:, par * 1024 + dc * 128:par * 1024 + (dc + 1) * 128],
                    start=(dc == 0), stop=False),
                  reads=[("wv", par), ("x1c", xpar)],
                  writes=[("pool", sl, 0)])
            E("tensor", lambda t, hp=hp, sl=sl: t.matmul(
                pool[sl][:, 0:128], rbf[0:1, 3088:3216],
                rbf[0:1, 2064 + hp * 128:2064 + (hp + 1) * 128],
                start=False, stop=True),
              reads=[("rbf",)], writes=[("pool", sl, 0)])
            for hh in range(2):
                E("vector", lambda v, par=par, tt=tt, hh=hh, sl=sl: v.tensor_copy(
                    v65[:, par * 4160 + tt * 130 + hh * 65:
                        par * 4160 + tt * 130 + hh * 65 + 64],
                    pool[sl][:, hh * 64:(hh + 1) * 64]),
                  reads=[("pool", sl, 0)], writes=[("v65", par, tt, hh)])

        def emit_QK(hp, tc):
            par = hp % 2
            xpar = tc % 2
            xco = xpar * 4096
            sl = next_slot()
            for dst, wsb, wnm, half, bcol in ((qsb, wqs, "wq", 0, hp * 2),
                                              (ksb, wks, "wk", 1, hp * 2 + 1)):
                for dc in range(DC):
                    E("tensor", lambda t, wsb=wsb, dc=dc, xco=xco, par=par, sl=sl,
                      half=half:
                      t.matmul(pool[sl][:, half * 512:(half + 1) * 512],
                               wsb[:, par * 1024 + dc * 128:par * 1024 + (dc + 1) * 128],
                               x1c[:, xco + dc * 512:xco + (dc + 1) * 512],
                               start=(dc == 0), stop=(dc == DC - 1)),
                      reads=[(wnm, par), ("x1c", xpar)],
                      writes=[("pool", sl, half)])
                E("vector", lambda v, dst=dst, tc=tc, par=par, bcol=bcol, sl=sl,
                  half=half:
                  v.tensor_scalar(dst[:, par * 4096 + tc * 512:par * 4096 + (tc + 1) * 512],
                                  pool[sl][:, half * 512:(half + 1) * 512],
                                  bc[:, bcol:bcol + 1], None, ALU.add),
                  reads=[("pool", sl, half), ("bc",)],
                  writes=[("qk" + str(half), par, tc)])

        load_qkvw(0)
        for tc in range(NTC):
            stream_x1(0, tc)
            emit_QK(0, tc)
            for j in range(4):
                emit_V_tt(0, tc, j)

        for hp in range(HP):
            par = hp % 2
            vtasks = []
            if hp + 1 < HP:
                load_qkvw(hp + 1)
                for tc in range(NTC):
                    vtasks.append(("x1", tc, 0))
                    vtasks.append(("qk", tc, 0))
                    for j in range(4):
                        vtasks.append(("v", tc, j))
            vti = 0

            qb = par * 4096

            def emit_S(h2, qst, u, par=par, qb=qb):
                sl = next_slot()
                pslots[(h2, u % 8)] = sl
                for half in range(2):
                    kt = 2 * u + half
                    hb = 64 * h2
                    off = max(0, kt - 4 * qst) * 128
                    E("tensor", lambda t, h2=h2, kt=kt, half=half, hb=hb, qst=qst,
                      par=par, qb=qb, sl=sl, off=off:
                      t.matmul(
                        pool[sl][:, half * 512 + off:(half + 1) * 512],
                        ksb[hb:hb + 64,
                            qb + kt * 128:qb + (kt + 1) * 128],
                        qsb[hb:hb + 64, qb + qst * 512 + off:qb + (qst + 1) * 512],
                        start=True, stop=True),
                      reads=[("qk1", par, kt // 4), ("qk0", par, qst)],
                      writes=[("pool", sl, half)])

            def emit_mask_exp(h2, qst, u):
                sl = pslots[(h2, u % 8)]
                for half in range(2):
                    j = 2 * u + half - 4 * qst
                    if j >= 0:
                        off = 128 * j
                        E("vector", lambda v, sl=sl, half=half, j=j, off=off: v.tensor_add(
                            pool[sl][:, half * 512 + off:half * 512 + off + 128],
                            pool[sl][:, half * 512 + off:half * 512 + off + 128],
                            msk[:, j * 512 + off:j * 512 + off + 128]),
                          reads=[("msk",), ("pool", sl, half)],
                          writes=[("pool", sl, half)])
                pslot = h2 * 2 + u % 2
                E("scalar", lambda s, sl=sl, pslot=pslot: s.activation(
                    ptb[:, pslot * 1024:(pslot + 1) * 1024], pool[sl][:],
                    AF.Exp, bias=cm16[:], scale=0.125),
                  reads=[("pool", sl, 0), ("pool", sl, 1), ("cm16",)],
                  writes=[("ptb", pslot)])

            def emit_PV(h2, qst, u, opar, npair, par=par):
                pslot = h2 * 2 + u % 2
                for half in range(2):
                    kt = 2 * u + half
                    off = max(0, kt - 4 * qst) * 128
                    E("tensor", lambda t, h2=h2, kt=kt, half=half, opar=opar,
                      npair=npair, par=par, pslot=pslot, off=off: t.matmul(
                        pso[opar][:, off:512],
                        v65[:, par * 4160 + kt * 130 + h2 * 65:
                            par * 4160 + kt * 130 + h2 * 65 + 65],
                        ptb[:, pslot * 1024 + half * 512 + off:
                            pslot * 1024 + (half + 1) * 512],
                        start=(kt == 0), stop=(kt == 2 * npair - 1)),
                      reads=[("v65", par, kt, h2), ("v65o", par), ("ptb", pslot)],
                      writes=[("pso", opar)])

            def emit_norm(h2, qst, opar, hp=hp):
                rp = opar % 2
                E("vector", lambda v, opar=opar, rp=rp: v.tensor_copy(
                    srow[64:65, rp * 512:(rp + 1) * 512], pso[opar][64:65, :]),
                  reads=[("pso", opar)], writes=[("srow", rp)])
                E("vector", lambda v, opar=opar: v.tensor_copy(
                    ost[:, (opar % 2) * 512:(opar % 2 + 1) * 512],
                    pso[opar][0:64, :]),
                  reads=[("pso", opar)], writes=[("ost", opar % 2)])
                bsl = next_slot()
                E("tensor", lambda t, rp=rp, bsl=bsl: t.matmul(
                    pool[bsl][0:64, 0:512], onesbf[64:65, 0:64],
                    srow[64:65, rp * 512:(rp + 1) * 512],
                    start=True, stop=True),
                  reads=[("srow", rp), ("onesbf",)], writes=[("pool", bsl, 0)])
                E("scalar", lambda s, rp=rp, bsl=bsl: s.activation(
                    rinv[:, rp * 512:(rp + 1) * 512], pool[bsl][0:64, 0:512], AF.Ln),
                  reads=[("pool", bsl, 0)], writes=[("rinv", rp)])
                E("scalar", lambda s, rp=rp: s.activation(
                    rinv[:, rp * 512:(rp + 1) * 512],
                    rinv[:, rp * 512:(rp + 1) * 512], AF.Exp, scale=-1.0),
                  reads=[("rinv", rp)], writes=[("rinv", rp)])
                E("vector", lambda v, h2=h2, qst=qst, opar=opar, hp=hp, rp=rp:
                  v.tensor_mul(
                    cmb[64 * h2:64 * h2 + 64,
                        hp * 4096 + qst * 512:hp * 4096 + (qst + 1) * 512],
                    ost[:, rp * 512:(rp + 1) * 512],
                    rinv[:, rp * 512:(rp + 1) * 512]),
                  reads=[("ost", rp), ("rinv", rp)],
                  writes=[("cmb", hp, h2, qst)])

            rounds_total = sum(2 * qst + 2 for qst in range(NTC))
            ridx = 0
            uidx = 0
            for qst in range(NTC):
                npair = 2 * qst + 2
                op0 = uidx % 2; uidx += 1
                op1 = uidx % 2; uidx += 1
                emit_S(0, qst, 0)
                emit_S(1, qst, 0)
                for u in range(npair):
                    emit_mask_exp(0, qst, u)
                    emit_mask_exp(1, qst, u)
                    if u + 1 < npair:
                        emit_S(0, qst, u + 1)
                        emit_S(1, qst, u + 1)
                    emit_PV(0, qst, u, op0, npair)
                    emit_PV(1, qst, u, op1, npair)
                    ridx += 1
                    want = (ridx * len(vtasks)) // rounds_total if vtasks else 0
                    while vti < want:
                        kind, tc, j = vtasks[vti]
                        if kind == "x1":
                            stream_x1(hp + 1, tc)
                        elif kind == "qk":
                            emit_QK(hp + 1, tc)
                        else:
                            emit_V_tt(hp + 1, tc, j)
                        vti += 1
                emit_norm(0, qst, op0)
                emit_norm(1, qst, op1)
            while vti < len(vtasks):
                kind, tc, j = vtasks[vti]
                if kind == "x1":
                    stream_x1(hp + 1, tc)
                elif kind == "qk":
                    emit_QK(hp + 1, tc)
                else:
                    emit_V_tt(hp + 1, tc, j)
                vti += 1
        with nc.allow_low_precision("bf16 softmax denominators, within bf16-matmul tolerance"):
            em.run_block(nc)
        free(g_b)
        for pre in ("wq", "wk", "wv", "x1c", "qk0", "qk1", "v65", "v65o", "ptb",
                    "rinv", "srow", "ost", "pool", "pso", "onesbf"):
            em.retire_prefix(pre)

        g_c = []
        gg, wos = man("wos", [128, 8 * 1024], BF16); g_c.append(gg)
        gg, xsc = man("xsc", [128, 3 * 1024], F32); g_c.append(gg)
        gg, awc = man("awc", [128, 6 * 1024], F32); g_c.append(gg)
        gg, xhb = man("xhb", [128, 3 * 1024], BF16); g_c.append(gg)
        gg, xlb = man("xlb", [128, 3 * 1024], BF16); g_c.append(gg)
        gg, stg2 = man("stg2", [128, 2 * 4096], BF16); g_c.append(gg)
        gg, psc0 = manp("psc0", [128, 1024], F32); g_c.append(gg)
        gg, psc1 = manp("psc1", [128, 1024], F32); g_c.append(gg)
        gg, psc2 = manp("psc2", [128, 1024], F32); g_c.append(gg)
        psc = (psc0, psc1, psc2)

        DMA("gpsimd", lambda g: g.dma_start(
            wos[:].rearrange("p (c n) -> p c n", c=8),
            wo_d.rearrange("(c p) n -> p c n", p=128)), writes=[("wos",)])

        def c_stage1(tt):
            sp = tt % 3
            xa = xsc[:, sp * 1024:(sp + 1) * 1024]
            ao = awc[:, sp * 1024:(sp + 1) * 1024]
            DMA("sync", lambda s, xa=xa, tt=tt: s.dma_start(
                xa, x_d[tt * 128:(tt + 1) * 128, :]), writes=[("xC", sp)])
            for of2 in range(2):
                for cfc in range(DC):
                    E("tensor", lambda t, sp=sp, of2=of2, cfc=cfc, tt=tt: t.matmul(
                        psc[sp][:, of2 * 512:(of2 + 1) * 512],
                        cmb[:, cfc * 4096 + tt * 128:cfc * 4096 + (tt + 1) * 128],
                        wos[:, cfc * 1024 + of2 * 512:cfc * 1024 + (of2 + 1) * 512],
                        start=(cfc == 0), stop=(cfc == DC - 1)),
                      reads=[("cmb", cfc, 0, tt // 4), ("cmb", cfc, 1, tt // 4), ("wos",)],
                      writes=[("psc", sp, of2)])
            E("vector", lambda v, ao=ao, sp=sp: v.tensor_add(ao, psc[sp][:], obc[:]),
              reads=[("psc", sp, 0), ("psc", sp, 1), ("obc",)], writes=[("aoC", sp)])
            ln_stats(ao, [("aoC", sp)], sp)

        def c_stage2(tt):
            sp = tt % 3
            o = sp * 20
            sid = ("stats", sp)
            xa = xsc[:, sp * 1024:(sp + 1) * 1024]
            ao = awc[:, sp * 1024:(sp + 1) * 1024]
            tw = awc[:, (3 + sp) * 1024:(4 + sp) * 1024]
            rstd = stats[:, o + 15:o + 16]
            negmr = stats[:, o + 16:o + 17]
            E("scalar", lambda s, tw=tw, ao=ao, rstd=rstd, negmr=negmr: s.activation(
                tw, ao, AF.Identity, bias=negmr, scale=rstd),
              reads=[("aoC", sp), sid], writes=[("twC", sp)])
            E("vector", lambda v, ao=ao, xa=xa, tw=tw: v.tensor_add(ao, xa, tw),
              reads=[("xC", sp), ("twC", sp), ("aoC", sp)], writes=[("aoC", sp)])
            DMA("sync", lambda s, ao=ao, tt=tt: s.dma_start(
                xp_d[tt * 128:(tt + 1) * 128, :], ao), reads=[("aoC", sp)],
                writes=[("xp", tt)])
            ln_stats(ao, [("aoC", sp)], sp)

        def c_stage3a(tt):
            sp = tt % 3
            o = sp * 20
            sid = ("stats", sp)
            ao = awc[:, sp * 1024:(sp + 1) * 1024]
            tw = awc[:, (3 + sp) * 1024:(4 + sp) * 1024]
            rstd = stats[:, o + 15:o + 16]
            negmr = stats[:, o + 16:o + 17]
            E("scalar", lambda s, tw=tw, ao=ao, rstd=rstd, negmr=negmr, sp=sp, sid=sid: s.activation(
                tw, ao, AF.Identity, bias=negmr, scale=rstd),
              reads=[("aoC", sp), sid], writes=[("twC", sp)])
            xh = xhb[:, sp * 1024:(sp + 1) * 1024]
            xl = xlb[:, sp * 1024:(sp + 1) * 1024]
            E("scalar", lambda s, xh=xh, tw=tw: s.activation(xh, tw, AF.Copy),
              reads=[("twC", sp)], writes=[("xhC", sp)])
            E("vector", lambda v, xl=xl, tw=tw, xh=xh: v.tensor_sub(xl, tw, xh),
              reads=[("twC", sp), ("xhC", sp)], writes=[("xlC", sp)])

        def c_stage3b(tt):
            sp = tt % 3
            xh = xhb[:, sp * 1024:(sp + 1) * 1024]
            xl = xlb[:, sp * 1024:(sp + 1) * 1024]
            DMA("sync", lambda s, xh=xh, tt=tt: s.dma_start(
                x2fm[tt * 128:(tt + 1) * 128, :], xh),
                reads=[("xhC", sp)], writes=[("x2fm", tt)])
            DMA("sync", lambda s, xl=xl, tt=tt: s.dma_start(
                x2lo[tt * 128:(tt + 1) * 128, :], xl),
                reads=[("xlC", sp)], writes=[("x2lo", tt)])
            if tt % 4 == 3:
                tc = tt // 4
                spar = tc % 2
                rdep = [("x2fm", 4 * tc + k) for k in range(4)]
                for dc in range(DC):
                    DMA("sync", lambda s, tc=tc, dc=dc, spar=spar: s.dma_start(
                        stg2[:, spar * 4096 + dc * 512:spar * 4096 + (dc + 1) * 512],
                        x2fm[tc * 512:(tc + 1) * 512, dc * 128:(dc + 1) * 128],
                        transpose=True),
                        reads=rdep, writes=[("stg2", spar, dc)])
                    DMA("sync", lambda s, tc=tc, dc=dc, spar=spar: s.dma_start(
                        xfmd[dc * 128:(dc + 1) * 128, tc * 512:(tc + 1) * 512],
                        stg2[:, spar * 4096 + dc * 512:spar * 4096 + (dc + 1) * 512]),
                        reads=[("stg2", spar, dc)], writes=[("xfmd", dc, tc)])

        for i in range(NTT + 3):
            if i < NTT:
                c_stage1(i)
            if 1 <= i <= NTT:
                c_stage2(i - 1)
            if 2 <= i <= NTT + 1:
                c_stage3a(i - 2)
            if 3 <= i:
                c_stage3b(i - 3)
        em.run_block(nc)
        free(g_c)
        free(g_bc)
        for pre in ("wos", "xC", "aoC", "twC", "xqC", "xhC", "xlC", "stgC", "psc",
                    "ptc", "msk", "rbf", "obc", "cmb"):
            em.retire_prefix(pre)

        g_d0 = []
        gg, x2ch = man("x2ch", [128, 2 * 4096], BF16); g_d0.append(gg)
        gg, x2cl = man("x2cl", [128, 2 * 4096], BF16); g_d0.append(gg)
        gg, gwk = man("gwk", [128, 2 * 384], F32); g_d0.append(gg)
        gg, psg0 = manp("psg0", [128, 64], F32); g_d0.append(gg)
        gg, psg1 = manp("psg1", [128, 64], F32); g_d0.append(gg)
        gg, psct0 = manp("psct0", [16, 128], F32); g_d0.append(gg)
        gg, psct1 = manp("psct1", [16, 128], F32); g_d0.append(gg)
        psg = (psg0, psg1)
        psct = (psct0, psct1)

        def emit_gate_chain(tc, gp):
            g0 = gwk[:, gp * 384:(gp + 1) * 384]
            b3 = lambda cols, k: g0[:, cols[0]:cols[1]].rearrange(
                "p (a b) -> p a b", b=k)
            eg = (0, 64); orig = (64, 128); sbt = (128, 192); sbm = (192, 256)
            m8 = (256, 288); n8 = (288, 320)
            t4a = (320, 336); t4b = (336, 352); t4c = (352, 368)
            nm = (368, 372); ssum = (372, 376); rs = (376, 380); thr = (380, 384)
            psgr = [("psg", gp, j) for j in range(4)]

            def C2(c):
                return g0[:, c[0]:c[1]]

            def BCL(c, n, k):
                return C2(c).rearrange("p (a b) -> p a b", b=1).broadcast_to([128, n, k])

            def V(fn, r=()):
                E("vector", fn, reads=[gid] + list(r), writes=[gid])

            V(lambda v: v.reduce_max(b3(nm, 1), psg[gp][:].rearrange(
                "p (a b) -> p a b", b=16), AX.X, negate=True), r=psgr)
            for j in range(4):
                E("scalar", lambda s, j=j: s.activation(
                    C2(eg)[:, j * 16:(j + 1) * 16], psg[gp][:, j * 16:(j + 1) * 16],
                    AF.Exp, bias=C2(nm)[:, j:j + 1], accum_out=C2(ssum)[:, j:j + 1]),
                  reads=[gid] + psgr, writes=[gid])
            V(lambda v: v.reciprocal(C2(rs), C2(ssum)))
            V(lambda v: v.tensor_mul(b3(eg, 16), b3(eg, 16), BCL(rs, 4, 16)))
            V(lambda v: v.tensor_copy(C2(orig), C2(eg)))
            V(lambda v: v.tensor_add(b3(sbt, 16), b3(orig, 16),
                                     gbc[:, 0:16].rearrange("p (a b) -> p a b", a=1)
                                     .broadcast_to([128, 4, 16])), r=[("gbc",)])
            V(lambda v: v.tensor_tensor(b3(m8, 8), b3(sbt, 16)[:, :, 0::2],
                                        b3(sbt, 16)[:, :, 1::2], ALU.max))
            V(lambda v: v.tensor_tensor(b3(n8, 8), b3(sbt, 16)[:, :, 0::2],
                                        b3(sbt, 16)[:, :, 1::2], ALU.min))
            V(lambda v: v.tensor_add(b3(t4a, 4), b3(m8, 8)[:, :, 0::2],
                                     b3(m8, 8)[:, :, 1::2]))
            V(lambda v: v.tensor_add(b3(t4b, 4), b3(m8, 8)[:, :, 0::2],
                                     b3(n8, 8)[:, :, 0::2]))
            V(lambda v: v.tensor_add(b3(t4c, 4), b3(m8, 8)[:, :, 1::2],
                                     b3(n8, 8)[:, :, 1::2]))
            V(lambda v: v.tensor_tensor(C2(t4a), C2(t4a), C2(t4b), ALU.max))
            V(lambda v: v.tensor_tensor(C2(t4a), C2(t4a), C2(t4c), ALU.max))
            V(lambda v: v.tensor_tensor(b3(t4b, 4)[:, :, 0:2] if False else
                                        g0[:, 336:344].rearrange("p (a b) -> p a b", b=2),
                                        b3(t4a, 4)[:, :, 0::2],
                                        b3(t4a, 4)[:, :, 1::2], ALU.max))
            V(lambda v: v.tensor_tensor(g0[:, 344:352].rearrange("p (a b) -> p a b", b=2),
                                        b3(t4a, 4)[:, :, 0::2],
                                        b3(t4a, 4)[:, :, 1::2], ALU.min))
            V(lambda v: v.tensor_tensor(g0[:, 352:356].rearrange("p (a b) -> p a b", b=1),
                                        g0[:, 336:344].rearrange("p (a b) -> p a b", b=2)[:, :, 0:1],
                                        g0[:, 336:344].rearrange("p (a b) -> p a b", b=2)[:, :, 1:2],
                                        ALU.min))
            V(lambda v: v.tensor_tensor(g0[:, 356:360].rearrange("p (a b) -> p a b", b=1),
                                        g0[:, 344:352].rearrange("p (a b) -> p a b", b=2)[:, :, 0:1],
                                        g0[:, 344:352].rearrange("p (a b) -> p a b", b=2)[:, :, 1:2],
                                        ALU.max))
            V(lambda v: v.tensor_tensor(C2(thr), g0[:, 352:356], g0[:, 356:360], ALU.max))
            V(lambda v: v.tensor_tensor(b3(t4b, 4), b3(t4a, 4), BCL(thr, 4, 4), ALU.is_ge))
            V(lambda v: v.tensor_scalar(C2(t4b), C2(t4b), -1.0, 1e30, ALU.add, ALU.mult))
            V(lambda v: v.tensor_add(
                g0[:, sbm[0]:sbm[1]].rearrange("p (a g e) -> p a g e", a=4, g=4),
                g0[:, sbt[0]:sbt[1]].rearrange("p (a g e) -> p a g e", a=4, g=4),
                g0[:, t4b[0]:t4b[1]].rearrange("p (a g b) -> p a g b", a=4, b=1)
                .broadcast_to([128, 4, 4, 4])))
            V(lambda v: v.tensor_tensor(b3(m8, 8), b3(sbm, 16)[:, :, 0::2],
                                        b3(sbm, 16)[:, :, 1::2], ALU.max))
            V(lambda v: v.tensor_tensor(b3(n8, 8), b3(sbm, 16)[:, :, 0::2],
                                        b3(sbm, 16)[:, :, 1::2], ALU.min))
            V(lambda v: v.tensor_tensor(b3(t4a, 4), b3(m8, 8)[:, :, 0::2],
                                        b3(m8, 8)[:, :, 1::2], ALU.max))
            V(lambda v: v.tensor_tensor(b3(t4b, 4), b3(m8, 8)[:, :, 0::2],
                                        b3(m8, 8)[:, :, 1::2], ALU.min))
            V(lambda v: v.tensor_tensor(b3(t4c, 4), b3(n8, 8)[:, :, 0::2],
                                        b3(n8, 8)[:, :, 1::2], ALU.max))
            V(lambda v: v.tensor_tensor(C2(t4b), C2(t4b), C2(t4c), ALU.max))
            V(lambda v: v.tensor_tensor(g0[:, 352:360].rearrange("p (a b) -> p a b", b=2),
                                        b3(t4a, 4)[:, :, 0::2],
                                        b3(t4a, 4)[:, :, 1::2], ALU.max))
            V(lambda v: v.tensor_tensor(g0[:, 360:368].rearrange("p (a b) -> p a b", b=2),
                                        b3(t4a, 4)[:, :, 0::2],
                                        b3(t4a, 4)[:, :, 1::2], ALU.min))
            V(lambda v: v.tensor_tensor(g0[:, 320:328].rearrange("p (a b) -> p a b", b=2),
                                        b3(t4b, 4)[:, :, 0::2],
                                        b3(t4b, 4)[:, :, 1::2], ALU.max))
            V(lambda v: v.tensor_tensor(g0[:, 320:328].rearrange("p (a b) -> p a b", b=2),
                                        g0[:, 320:328].rearrange("p (a b) -> p a b", b=2),
                                        g0[:, 360:368].rearrange("p (a b) -> p a b", b=2),
                                        ALU.max))
            V(lambda v: v.tensor_tensor(g0[:, 328:332].rearrange("p (a b) -> p a b", b=1),
                                        g0[:, 352:360].rearrange("p (a b) -> p a b", b=2)[:, :, 0:1],
                                        g0[:, 352:360].rearrange("p (a b) -> p a b", b=2)[:, :, 1:2],
                                        ALU.min))
            V(lambda v: v.tensor_tensor(g0[:, 332:336].rearrange("p (a b) -> p a b", b=1),
                                        g0[:, 320:328].rearrange("p (a b) -> p a b", b=2)[:, :, 0:1],
                                        g0[:, 320:328].rearrange("p (a b) -> p a b", b=2)[:, :, 1:2],
                                        ALU.max))
            V(lambda v: v.tensor_tensor(C2(thr), g0[:, 328:332], g0[:, 332:336], ALU.max))
            V(lambda v: v.tensor_tensor(b3(sbm, 16), b3(sbm, 16), BCL(thr, 4, 16),
                                        ALU.is_ge))
            E("vector", lambda v, tc=tc: v.tensor_mul(
                ctm[:, tc * 64:(tc + 1) * 64], C2(orig), C2(sbm)),
              reads=[gid], writes=[("ctm", tc)])

        for tc in range(NTC):
            xpar = tc % 2
            rdepl = [("x2lo", 4 * tc + k) for k in range(4)]
            xfmd_r = xfmd.rearrange("(c p) t -> p c t", p=128)
            DMA("sync", lambda s, xpar=xpar, tc=tc, xfmd_r=xfmd_r: s.dma_start(
                x2ch[:, xpar * 4096:(xpar + 1) * 4096].rearrange("p (c t) -> p c t", c=8),
                xfmd_r[:, :, tc * 512:(tc + 1) * 512]),
                reads=[("xfmd", dc, tc) for dc in range(DC)],
                writes=[("x2ch", xpar, dc) for dc in range(DC)])
            for dc in range(DC):
                DMA("sync", lambda s, xpar=xpar, tc=tc, dc=dc: s.dma_start(
                    x2cl[:, xpar * 4096 + dc * 512:xpar * 4096 + (dc + 1) * 512],
                    x2lo[tc * 512:(tc + 1) * 512, dc * 128:(dc + 1) * 128],
                    transpose=True),
                    reads=rdepl, writes=[("x2cl", xpar, dc)])
            gp = tc % 2
            gid = ("gwk", gp)
            for j in range(4):
                idx = 0
                for xsrc, xnm, grhs, gid_ in ((x2ch, "x2ch", gwhi, ("gwhi",)),
                                              (x2cl, "x2cl", gwhi, ("gwhi",)),
                                              (x2ch, "x2ch", gwlo, ("gwlo",))):
                    for dc in range(DC):
                        E("tensor", lambda t, xsrc=xsrc, xpar=xpar, dc=dc, j=j,
                          grhs=grhs, gp=gp, idx=idx: t.matmul(
                            psg[gp][:, j * 16:(j + 1) * 16],
                            xsrc[:, xpar * 4096 + dc * 512 + j * 128:
                                 xpar * 4096 + dc * 512 + (j + 1) * 128],
                            grhs[:, dc * 16:(dc + 1) * 16],
                            start=(idx == 0), stop=(idx == 23)),
                          reads=[(xnm, xpar, dc), gid_], writes=[("psg", gp, j)])
                        idx += 1
            emit_gate_chain(tc, gp)
            for j in range(4):
                tt = tc * 4 + j
                E("tensor", lambda t, gp=gp, tc=tc, j=j: t.transpose(
                    psct[gp][:], ctm[:, tc * 64 + j * 16:tc * 64 + (j + 1) * 16],
                    eyef[:]),
                  reads=[("ctm", tc), ("eyef",)], writes=[("psct", gp)])
                E("scalar", lambda s, gp=gp, tt=tt: s.activation(
                    ctT[:, tt * 128:(tt + 1) * 128], psct[gp][:], AF.Copy),
                  reads=[("psct", gp)], writes=[("ctT", tt)])
        em.run_block(nc)
        free(g_d0)
        for pre in ("x2ch", "x2cl", "gwk", "psg", "psct"):
            em.retire_prefix(pre)

        g_d = []
        gg, ysb = man("ysb", [128, 16 * 1024], F32); g_d.append(gg)
        gg, x2s = man("x2s", [128, 2 * 4096], BF16); g_d.append(gg)
        gg, hsh = man("hsh", [128, 8 * 512], BF16); g_d.append(gg)
        gg, slb = man("slb", [128, 2 * 512], F32); g_d.append(gg)
        gg, hxb = man("hxb", [128, 2 * 2048], BF16); g_d.append(gg)
        gg, yos = man("yos", [128, 2 * 1024], F32); g_d.append(gg)
        gg, ewk = man("ewk", [128, 6 * 1024], F32); g_d.append(gg)
        gg, psh1 = manp("psh1", [128, 512], F32); g_d.append(gg)
        gg, psh3 = manp("psh3", [128, 512], F32); g_d.append(gg)
        gg, psy0 = manp("psy0", [128, 1024], F32); g_d.append(gg)
        gg, psy1 = manp("psy1", [128, 1024], F32); g_d.append(gg)
        psy = (psy0, psy1)
        x2cnt = [0]

        xfmd_r = xfmd.rearrange("(c p) t -> p c t", p=128)

        def stream_x2(tcg):
            par = x2cnt[0] % 2
            x2cnt[0] += 1
            DMA("sync", lambda s, par=par, tcg=tcg: s.dma_start(
                x2s[:, par * 4096:(par + 1) * 4096].rearrange("p (c t) -> p c t", c=8),
                xfmd_r[:, :, tcg * 512:(tcg + 1) * 512]),
                reads=[("xfmd", dc, tcg) for dc in range(DC)], writes=[("x2s", par)])
            return par

        for half in range(2):
            g_ws = []
            gg, ws1s = man(f"ws1s{half}", [128, 8 * 1024], BF16); g_ws.append(gg)
            gg, ws3s = man(f"ws3s{half}", [128, 8 * 1024], BF16); g_ws.append(gg)
            gg, ws2s = man(f"ws2s{half}", [128, 8 * 1024], BF16); g_ws.append(gg)
            for nm, wsb, wd in (("ws1", ws1s, ws1_d), ("ws3", ws3s, ws3_d),
                                ("ws2", ws2s, ws2_d)):
                DMA("gpsimd", lambda g, wsb=wsb, wd=wd: g.dma_start(
                    wsb[:].rearrange("p (c n) -> p c n", c=8),
                    wd.rearrange("(c p) n -> p c n", p=128)),
                    writes=[(nm, half)])
            for tc4 in range(4):
                tcg = half * 4 + tc4
                xpar = stream_x2(tcg)
                xo = xpar * 4096
                for ifc in range(8):
                    E_h1 = [("ws1", half), ("x2s", xpar)]
                    for dc in range(DC):
                        E("tensor", lambda t, ifc=ifc, dc=dc, xo=xo: t.matmul(
                            psh1[:], ws1s[:, dc * 1024 + ifc * 128:dc * 1024 + (ifc + 1) * 128],
                            x2s[:, xo + dc * 512:xo + (dc + 1) * 512],
                            start=(dc == 0), stop=(dc == DC - 1)),
                          reads=E_h1, writes=[("psh1",)])
                    sp = ifc % 2
                    E("scalar", lambda s, sp=sp, ifc=ifc: s.activation(
                        slb[:, sp * 512:(sp + 1) * 512], psh1[:], AF.Silu,
                        bias=bc[:, 144 + ifc:145 + ifc]),
                      reads=[("psh1",), ("bc",)], writes=[("slb", sp)])
                    for dc in range(DC):
                        E("tensor", lambda t, ifc=ifc, dc=dc, xo=xo: t.matmul(
                            psh3[:], ws3s[:, dc * 1024 + ifc * 128:dc * 1024 + (ifc + 1) * 128],
                            x2s[:, xo + dc * 512:xo + (dc + 1) * 512],
                            start=(dc == 0), stop=(dc == DC - 1)),
                          reads=[("ws3", half), ("x2s", xpar)], writes=[("psh3",)])
                    E("vector", lambda v, ifc=ifc, sp=sp: v.scalar_tensor_tensor(
                        hsh[:, ifc * 512:(ifc + 1) * 512], psh3[:],
                        bc[:, 152 + ifc:153 + ifc], slb[:, sp * 512:(sp + 1) * 512],
                        ALU.add, ALU.mult),
                      reads=[("psh3",), ("bc",), ("slb", sp)], writes=[("hsh", ifc)])
                for j in range(4):
                    ttl = tc4 * 4 + j
                    tt = half * 16 + ttl
                    jp = j % 2
                    for of2 in range(2):
                        for ifc in range(8):
                            E("tensor", lambda t, jp=jp, of2=of2, ifc=ifc, j=j: t.matmul(
                                psy[jp][:, of2 * 512:(of2 + 1) * 512],
                                hsh[:, ifc * 512 + j * 128:ifc * 512 + (j + 1) * 128],
                                ws2s[:, ifc * 1024 + of2 * 512:ifc * 1024 + (of2 + 1) * 512],
                                start=(ifc == 0), stop=False),
                              reads=[("hsh", ifc), ("ws2", half)],
                              writes=[("psy", jp, of2)])
                        E("tensor", lambda t, jp=jp, of2=of2, tt=tt: t.matmul(
                            psy[jp][:, of2 * 512:(of2 + 1) * 512],
                            ctT[:, tt * 128:(tt + 1) * 128],
                            be2b[:, of2 * 512:(of2 + 1) * 512],
                            start=False, stop=True),
                          reads=[("ctT", tt), ("be2b",)], writes=[("psy", jp, of2)])
                    E("vector", lambda v, ttl=ttl, jp=jp: v.tensor_add(
                        ysb[:, ttl * 1024:(ttl + 1) * 1024], psy[jp][:], sbc[:]),
                      reads=[("psy", jp, 0), ("psy", jp, 1), ("sbc",)],
                      writes=[("y", ttl)])
            em.run_block(nc)
            free(g_ws)
            em.retire([("ws1", half), ("ws3", half), ("ws2", half)])

            g_we = []
            gg, we1s = man(f"we1s{half}", [128, 2 * 4096], BF16); g_we.append(gg)
            gg, we3s = man(f"we3s{half}", [128, 2 * 4096], BF16); g_we.append(gg)
            gg, we2s = man(f"we2s{half}", [128, 2 * 4096], BF16); g_we.append(gg)

            def load_expert(e):
                wp = e % 2
                for nm, wsb, wd in (("we1", we1s, we1_d), ("we3", we3s, we3_d)):
                    DMA("gpsimd", lambda g, wsb=wsb, wd=wd, wp=wp, e=e: g.dma_start(
                        wsb[:, wp * 4096:(wp + 1) * 4096].rearrange("p (c n) -> p c n", c=8),
                        wd[e * D:(e + 1) * D, :].rearrange("(c p) n -> p c n", p=128)),
                        writes=[(nm, wp)])
                DMA("gpsimd", lambda g, wp=wp, e=e: g.dma_start(
                    we2s[:, wp * 4096:(wp + 1) * 4096].rearrange("p (c n) -> p c n", c=4),
                    we2_d[e * IN:(e + 1) * IN, :].rearrange("(c p) n -> p c n", p=128)),
                    writes=[("we2", wp)])

            load_expert(0)
            load_expert(1)
            for e in range(NE):
                wp = e % 2
                wo_ = wp * 4096
                for tc4 in range(4):
                    tcg = half * 4 + tc4
                    xpar = stream_x2(tcg)
                    xo = xpar * 4096
                    for ifc in range(4):
                        for dc in range(DC):
                            E("tensor", lambda t, ifc=ifc, dc=dc, xo=xo, wo_=wo_: t.matmul(
                                psh1[:],
                                we1s[:, wo_ + dc * 512 + ifc * 128:wo_ + dc * 512 + (ifc + 1) * 128],
                                x2s[:, xo + dc * 512:xo + (dc + 1) * 512],
                                start=(dc == 0), stop=(dc == DC - 1)),
                              reads=[("we1", wp), ("x2s", xpar)], writes=[("psh1",)])
                        sp = ifc % 2
                        E("scalar", lambda s, sp=sp, e=e, ifc=ifc: s.activation(
                            slb[:, sp * 512:(sp + 1) * 512], psh1[:], AF.Silu,
                            bias=bc[:, 16 + e * 4 + ifc:17 + e * 4 + ifc]),
                          reads=[("psh1",), ("bc",)], writes=[("slb", sp)])
                        for dc in range(DC):
                            E("tensor", lambda t, ifc=ifc, dc=dc, xo=xo, wo_=wo_: t.matmul(
                                psh3[:],
                                we3s[:, wo_ + dc * 512 + ifc * 128:wo_ + dc * 512 + (ifc + 1) * 128],
                                x2s[:, xo + dc * 512:xo + (dc + 1) * 512],
                                start=(dc == 0), stop=(dc == DC - 1)),
                              reads=[("we3", wp), ("x2s", xpar)], writes=[("psh3",)])
                        hxp = tc4 % 2
                        E("vector", lambda v, ifc=ifc, sp=sp, e=e, hxp=hxp: v.scalar_tensor_tensor(
                            hxb[:, hxp * 2048 + ifc * 512:hxp * 2048 + (ifc + 1) * 512],
                            psh3[:], bc[:, 80 + e * 4 + ifc:81 + e * 4 + ifc],
                            slb[:, sp * 512:(sp + 1) * 512], ALU.add, ALU.mult),
                          reads=[("psh3",), ("bc",), ("slb", sp)],
                          writes=[("hx", hxp, ifc)])
                    for j in range(4):
                        ttl = tc4 * 4 + j
                        tt = half * 16 + ttl
                        jp = j % 2
                        for of2 in range(2):
                            for ifc in range(4):
                                hxp = tc4 % 2
                                E("tensor", lambda t, jp=jp, of2=of2, ifc=ifc, j=j,
                                  wo_=wo_, hxp=hxp: t.matmul(
                                    psy[jp][:, of2 * 512:(of2 + 1) * 512],
                                    hxb[:, hxp * 2048 + ifc * 512 + j * 128:
                                        hxp * 2048 + ifc * 512 + (j + 1) * 128],
                                    we2s[:, wo_ + ifc * 1024 + of2 * 512:
                                         wo_ + ifc * 1024 + (of2 + 1) * 512],
                                    start=(ifc == 0), stop=(ifc == 3)),
                                  reads=[("hx", hxp, ifc), ("we2", wp)],
                                  writes=[("psy", jp, of2)])
                        E("scalar", lambda s, jp=jp, tt=tt, e=e: s.activation(
                            yos[:, jp * 1024:(jp + 1) * 1024], psy[jp][:],
                            AF.Identity, scale=ctm[:, tt * 16 + e:tt * 16 + e + 1]),
                          reads=[("psy", jp, 0), ("psy", jp, 1), ("ctm", tt // 4)],
                          writes=[("yos", jp)])
                        E("vector", lambda v, ttl=ttl, jp=jp: v.tensor_add(
                            ysb[:, ttl * 1024:(ttl + 1) * 1024],
                            ysb[:, ttl * 1024:(ttl + 1) * 1024],
                            yos[:, jp * 1024:(jp + 1) * 1024]),
                          reads=[("yos", jp), ("y", ttl)], writes=[("y", ttl)])
                if e + 2 < NE:
                    load_expert(e + 2)
            if DBG:
                DMA("sync", lambda s, half=half: s.dma_start(
                    dbgy[half * 128:(half + 1) * 128, :], ysb[:]),
                    reads=[("y", t_) for t_ in range(16)])
            for ttl in range(16):
                tt = half * 16 + ttl
                sp = ttl % 2
                y2 = ewk[:, sp * 1024:(sp + 1) * 1024]
                xpl = ewk[:, (2 + sp) * 1024:(3 + sp) * 1024]
                ob = ewk[:, (4 + sp) * 1024:(5 + sp) * 1024]
                ysl = ysb[:, ttl * 1024:(ttl + 1) * 1024]
                DMA("sync", lambda s, xpl=xpl, tt=tt: s.dma_start(
                    xpl, xp_d[tt * 128:(tt + 1) * 128, :]),
                    reads=[("xp", tt)], writes=[("xpl", sp)])
                rstd, negmr, sid = ln_stats(ysl, [("y", ttl)], sp)
                E("scalar", lambda s, y2=y2, ysl=ysl, rstd=rstd, negmr=negmr: s.activation(
                    y2, ysl, AF.Identity, bias=negmr, scale=rstd),
                  reads=[("y", ttl), sid], writes=[("y2", sp)])
                E("vector", lambda v, ob=ob, xpl=xpl, y2=y2: v.tensor_add(ob, xpl, y2),
                  reads=[("xpl", sp), ("y2", sp)], writes=[("ob", sp)])
                DMA("sync", lambda s, ob=ob, tt=tt: s.dma_start(
                    out_d[tt * 128:(tt + 1) * 128, :], ob),
                    reads=[("ob", sp)], writes=[("outd", tt)])
            if half == 1:
                em.final_waits()
            em.run_block(nc)
            free(g_we)
            for pre in ("we1", "we3", "we2", "y", "yos", "hx", "slb", "hsh",
                        "y2", "xpl", "ob"):
                em.retire_prefix(pre)
        free(g_d)

    return nc



def kernel(**inputs):
    x = np.ascontiguousarray(np.asarray(inputs["L0_attn_encodings"], dtype=np.float32))
    attn_W = np.asarray(inputs["L0_attn_attn_W"], dtype=np.float32)
    attn_b = np.asarray(inputs["L0_attn_attn_b"], dtype=np.float32)
    out_W = np.asarray(inputs["L0_attn_out_W"], dtype=np.float32)
    out_b = np.asarray(inputs["L0_attn_out_b"], dtype=np.float32)
    gate_w = np.asarray(inputs["L0_ffn_gate_w"], dtype=np.float32)
    gate_b = np.asarray(inputs["L0_ffn_gate_b"], dtype=np.float32)
    we1 = np.asarray(inputs["L0_ffn_We1"], dtype=np.float32)
    be1 = np.asarray(inputs["L0_ffn_be1"], dtype=np.float32)
    we2 = np.asarray(inputs["L0_ffn_We2"], dtype=np.float32)
    be2 = np.asarray(inputs["L0_ffn_be2"], dtype=np.float32)
    we3 = np.asarray(inputs["L0_ffn_We3"], dtype=np.float32)
    be3 = np.asarray(inputs["L0_ffn_be3"], dtype=np.float32)
    ws1 = np.asarray(inputs["L0_ffn_ws1"], dtype=np.float32)
    bs1 = np.asarray(inputs["L0_ffn_bs1"], dtype=np.float32)
    ws2 = np.asarray(inputs["L0_ffn_ws2"], dtype=np.float32)
    bs2 = np.asarray(inputs["L0_ffn_bs2"], dtype=np.float32)
    ws3 = np.asarray(inputs["L0_ffn_ws3"], dtype=np.float32)
    bs3 = np.asarray(inputs["L0_ffn_bs3"], dtype=np.float32)

    W3 = attn_W.reshape(D, H, 3, HS)
    wq = np.ascontiguousarray(W3[:, :, 0, :].reshape(D, D))
    wk = np.ascontiguousarray(W3[:, :, 1, :].reshape(D, D))
    wv = np.ascontiguousarray(W3[:, :, 2, :].reshape(D, D))
    b3 = attn_b.reshape(H, 3, HS)
    bq = np.ascontiguousarray(b3[:, 0, :].reshape(D))
    bk = np.ascontiguousarray(b3[:, 1, :].reshape(D))
    bv = np.ascontiguousarray(b3[:, 2, :].reshape(D))

    biascols = np.zeros((128, 160), np.float32)
    for hp in range(HP):
        biascols[:, hp * 2 + 0] = bq[hp * 128:(hp + 1) * 128]
        biascols[:, hp * 2 + 1] = bk[hp * 128:(hp + 1) * 128]
    for e in range(NE):
        for ifc in range(4):
            biascols[:, 16 + e * 4 + ifc] = be1[e, ifc * 128:(ifc + 1) * 128]
            biascols[:, 80 + e * 4 + ifc] = be3[e, ifc * 128:(ifc + 1) * 128]
    for ifc in range(8):
        biascols[:, 144 + ifc] = bs1[ifc * 128:(ifc + 1) * 128]
        biascols[:, 152 + ifc] = bs3[ifc * 128:(ifc + 1) * 128]

    rowvec = np.zeros((1, 3216), np.float32)
    rowvec[0, 0:1024] = out_b
    rowvec[0, 1024:2048] = bs2
    rowvec[0, 2048:2064] = gate_b
    rowvec[0, 2064:3088] = bv
    rowvec[0, 3088:3216] = 1.0

    pp, ff = np.meshgrid(np.arange(128), np.arange(512), indexing="ij")
    masks = np.zeros((128, 2048), np.float32)
    for j in range(4):
        masks[:, j * 512:(j + 1) * 512] = np.where(pp + 128 * j <= ff, 0.0, -80000.0)
    eye = np.eye(128, dtype=np.float32)

    shared = {
        "wq": wq, "wk": wk, "wv": wv, "wo": out_W,
        "gwt": np.ascontiguousarray(gate_w.T),
        "we1": np.ascontiguousarray(we1.reshape(NE * D, IN)),
        "we3": np.ascontiguousarray(we3.reshape(NE * D, IN)),
        "we2": np.ascontiguousarray(we2.reshape(NE * IN, D)),
        "ws1": ws1, "ws3": ws3, "ws2": ws2,
        "be2": be2, "biascols": biascols, "rowvec": rowvec,
        "masks": masks, "eye": eye,
    }
    in_maps = [dict(shared, x=np.ascontiguousarray(x[c])) for c in range(8)]

    nc = build()
    res = run_bass_kernel_spmd(nc, in_maps, core_ids=list(range(8)))
    if DBG:
        global _last_results
        _last_results = res.results
    return np.stack([r["out"] for r in res.results], axis=0)


# version 31
# speedup vs baseline: 1.0548x; 1.0548x over previous
import numpy as np
import concourse.bass as bass
import concourse.mybir as mybir
from concourse.bass_utils import run_bass_kernel_spmd
from contextlib import ExitStack

F32 = mybir.dt.float32
BF16 = mybir.dt.bfloat16
AF = mybir.ActivationFunctionType
ALU = mybir.AluOpType
AX = mybir.AxisListType

T, D, H, HS, HP, DC, NTT, NTC = 4096, 1024, 16, 64, 8, 8, 32, 8
NE, IN = 16, 512

ENGINES = ("sync", "scalar", "vector", "tensor", "gpsimd")
NSLOT = {"sync": 12, "gpsimd": 8}

DBG = False


class Em:

    def __init__(self):
        self.tapes = {e: [] for e in ENGINES}
        self.ecnt = {e: 0 for e in ENGINES}
        self.slot_uses = {q: [0] * NSLOT[q] for q in NSLOT}
        self.dcnt = {q: 0 for q in NSLOT}
        self.know = {e: {} for e in ENGINES}
        self.snap = {}
        self.bufs = {}
        self.legacy = {}
        self.sems = {}

    def _buf(self, b):
        s = self.bufs.get(b)
        if s is None:
            s = {"w": None, "r": [tuple(kv) for kv in self.legacy.items()]}
            self.bufs[b] = s
        return s

    def _needs(self, eng, reads, writes):
        need = {}
        k = self.know[eng]

        def add(tok):
            if tok is None:
                return
            s, v = tok
            if eng == "tensor" and s == "e_tensor":
                return
            if k.get(s, 0) >= v:
                return
            if need.get(s, 0) < v:
                need[s] = v

        for b in reads:
            add(self._buf(b)["w"])
        for b in writes:
            st = self._buf(b)
            add(st["w"])
            for t in st["r"]:
                add(t)
        return need

    def _absorb(self, eng, need):
        k = self.know[eng]
        for s, v in need.items():
            sn = self.snap.get((s, v))
            if sn:
                for s2, v2 in sn.items():
                    if k.get(s2, 0) < v2:
                        k[s2] = v2
            if k.get(s, 0) < v:
                k[s] = v

    def _record(self, tok, eng, reads, writes):
        sn = dict(self.know[eng])
        sn[tok[0]] = max(sn.get(tok[0], 0), tok[1])
        self.snap[tok] = sn
        for b in reads:
            self._buf(b)["r"].append(tok)
        for b in writes:
            st = self._buf(b)
            st["w"] = tok
            st["r"] = []

    def emit(self, eng, fn, reads=(), writes=()):
        need = self._needs(eng, reads, writes)
        self._absorb(eng, need)
        self.ecnt[eng] += 1
        tok = ("e_" + eng, self.ecnt[eng])
        if eng == "tensor":
            self.know[eng]["e_tensor"] = self.ecnt["tensor"]
        self._record(tok, eng, reads, writes)
        waits = sorted(need.items())
        sems = self.sems

        def step(E, waits=waits, fn=fn, tok=tok):
            for s, v in waits:
                E.wait_ge(sems[s], v)
            fn(E).then_inc(sems[tok[0]], 1)

        self.tapes[eng].append(step)
        return tok

    def dma(self, q, fn, reads=(), writes=()):
        need = self._needs(q, reads, writes)
        self._absorb(q, need)
        slot = self.dcnt[q] % NSLOT[q]
        self.dcnt[q] += 1
        self.slot_uses[q][slot] += 1
        tok = (f"d_{q}{slot}", 16 * self.slot_uses[q][slot])
        self._record(tok, q, reads, writes)
        waits = sorted(need.items())
        sems = self.sems

        def step(E, waits=waits, fn=fn, tok=tok):
            for s, v in waits:
                E.wait_ge(sems[s], v)
            fn(E).then_inc(sems[tok[0]], 16)

        self.tapes[q].append(step)
        return tok

    def retire(self, buf_ids):
        for b in buf_ids:
            st = self.bufs.pop(b, None)
            if st is None:
                continue
            toks = list(st["r"])
            if st["w"] is not None:
                toks.append(st["w"])
            for s, v in toks:
                if self.legacy.get(s, 0) < v:
                    self.legacy[s] = v

    def retire_prefix(self, prefix):
        ids = [b for b in self.bufs
               if isinstance(b, tuple) and len(b) > 0 and b[0] == prefix]
        self.retire(ids)

    def final_waits(self):
        fin = {}
        for e in ENGINES:
            if self.ecnt[e]:
                fin["e_" + e] = self.ecnt[e]
        for q in NSLOT:
            for sl, u in enumerate(self.slot_uses[q]):
                if u:
                    fin[f"d_{q}{sl}"] = 16 * u
        sems = self.sems

        def step(E, fin=sorted(fin.items())):
            for s, v in fin:
                E.wait_ge(sems[s], v)

        self.tapes["sync"].append(step)

    def run_block(self, nc):
        with nc.Block() as blk:
            for ename in ENGINES:
                steps = self.tapes[ename]
                if steps:
                    getattr(blk, ename)(
                        lambda eng, steps=steps: [f(eng) for f in steps])
        self.tapes = {e: [] for e in ENGINES}


def build():
    nc = bass.Bass("TRN2", target_bir_lowering=False, debug=False)
    em = Em()

    dr = lambda n, s, d, k: nc.dram_tensor(n, s, d, kind=k).ap()
    x_d = dr("x", [T, D], F32, "ExternalInput")
    wq_d = dr("wq", [D, D], F32, "ExternalInput")
    wk_d = dr("wk", [D, D], F32, "ExternalInput")
    wv_d = dr("wv", [D, D], F32, "ExternalInput")
    wo_d = dr("wo", [D, D], F32, "ExternalInput")
    gw_d = dr("gwt", [D, 16], F32, "ExternalInput")
    we1_d = dr("we1", [NE * D, IN], F32, "ExternalInput")
    we3_d = dr("we3", [NE * D, IN], F32, "ExternalInput")
    we2_d = dr("we2", [NE * IN, D], F32, "ExternalInput")
    ws1_d = dr("ws1", [D, D], F32, "ExternalInput")
    ws3_d = dr("ws3", [D, D], F32, "ExternalInput")
    ws2_d = dr("ws2", [D, D], F32, "ExternalInput")
    be2_d = dr("be2", [NE, D], F32, "ExternalInput")
    bc_d = dr("biascols", [128, 160], F32, "ExternalInput")
    rv_d = dr("rowvec", [1, 3216], F32, "ExternalInput")
    mk_d = dr("masks", [128, 512], F32, "ExternalInput")
    ey_d = dr("eye", [128, 128], F32, "ExternalInput")
    out_d = dr("out", [T, D], F32, "ExternalOutput")
    scratch_kind = "ExternalOutput" if DBG else "Internal"
    x1fm = dr("x1fm", [D, T], BF16, scratch_kind)
    dbgc = dr("dbgctm", [128, 512], F32, scratch_kind) if DBG else None
    dbgy = dr("dbgy", [256, 16384], F32, scratch_kind) if DBG else None
    xp_d = dr("xp", [T, D], F32, scratch_kind)
    x2fm = dr("x2tm", [T, D], BF16, scratch_kind)
    x2lo = dr("x2lotm", [T, D], BF16, scratch_kind)
    xfmd = dr("xfmd", [D, T], BF16, scratch_kind)

    stack = ExitStack()
    with stack:
        sem_names = ["e_" + e for e in ENGINES]
        for q in NSLOT:
            sem_names += [f"d_{q}{i}" for i in range(NSLOT[q])]
        for s in sem_names:
            em.sems[s] = stack.enter_context(nc.semaphore(s))

        SB = lambda n, sh, dt: stack.enter_context(nc.sbuf_tensor(n, sh, dt))
        bc = SB("bc", [128, 160], F32)
        eyef = SB("eyef", [128, 128], F32)
        eyeb = SB("eyeb", [128, 128], BF16)
        onef = SB("onef", [128, 128], F32)
        cm16 = SB("cm16", [128, 1], F32)
        ceps = SB("ceps", [128, 1], F32)
        gwf = SB("gwf", [128, 128], F32)
        gwhi = SB("gwhi", [128, 128], BF16)
        gwlo = SB("gwlo", [128, 128], BF16)
        sbc = SB("sbc", [128, 1024], F32)
        gbc = SB("gbc", [128, 16], F32)
        ctm = SB("ctm", [128, 512], F32)
        ctT = SB("ctT", [16, 4096], BF16)
        be2b = SB("be2b", [16, 1024], BF16)
        stats = SB("stats", [128, 80], F32)

        g_bc = []

        def man(n, sh, dt):
            g = nc.sbuf_tensor(n, sh, dt)
            t = g.__enter__()
            return g, t

        def manp(n, sh, dt):
            g = nc.psum_tensor(n, sh, dt)
            t = g.__enter__()
            return g, t

        def free(guards):
            for g in reversed(guards):
                g.__exit__(None, None, None)

        gg, msk = man("msk", [128, 512], F32); g_bc.append(gg)
        gg, rbf = man("rbf", [1, 1152], BF16); g_bc.append(gg)
        gg, obc = man("obc", [128, 1024], F32); g_bc.append(gg)
        gg, cmb = man("cmb", [128, 8 * 4096], BF16); g_bc.append(gg)

        g_rv = []
        gg, rv = man("rv", [1, 3216], F32); g_rv.append(gg)
        g_sp = []
        gg, psu = manp("psu", [128, 1024], F32); g_sp.append(gg)

        E, DMA = em.emit, em.dma
        DMA("sync", lambda s: s.dma_start(bc[:], bc_d), writes=[("bc",)])
        DMA("sync", lambda s: s.dma_start(msk[:], mk_d), writes=[("msk",)])
        DMA("sync", lambda s: s.dma_start(eyef[:], ey_d), writes=[("eyef",)])
        DMA("sync", lambda s: s.dma_start(rv[:], rv_d), writes=[("rv",)])
        DMA("gpsimd", lambda g: g.dma_start(eyeb[:], ey_d), writes=[("eyeb",)])
        DMA("gpsimd", lambda g: g.dma_start(rbf[:], rv_d[0:1, 2064:3216]), writes=[("rbf",)])
        DMA("gpsimd", lambda g: g.dma_start(be2b[:], be2_d), writes=[("be2b",)])
        DMA("gpsimd", lambda g: g.dma_start(
            gwf[:].rearrange("p (c n) -> p c n", c=8),
            gw_d.rearrange("(c p) n -> p c n", p=128)), writes=[("gwf",)])
        E("vector", lambda v: v.memset(onef[:], 1.0), writes=[("onef",)])
        E("vector", lambda v: v.memset(cm16[:], -16.0), writes=[("cm16",)])
        E("vector", lambda v: v.memset(ceps[:], 1e-5), writes=[("ceps",)])
        E("scalar", lambda s: s.activation(gwhi[:], gwf[:], AF.Copy),
          reads=[("gwf",)], writes=[("gwhi",)])
        E("vector", lambda v: v.tensor_sub(gwlo[:], gwf[:], gwhi[:]),
          reads=[("gwf",), ("gwhi",)], writes=[("gwlo",)])
        for name, dst, lo, n in (("obc", obc, 0, 1024), ("sbc", sbc, 1024, 1024),
                                 ("gbc", gbc, 2048, 16)):
            for c0 in range(0, n, 512):
                w = min(512, n - c0)
                E("tensor", lambda t, lo=lo, c0=c0, w=w: t.matmul(
                    psu[:, c0:c0 + w], onef[0:1, 0:128],
                    rv[0:1, lo + c0:lo + c0 + w], start=True, stop=True),
                  reads=[("rv",), ("onef",)], writes=[("psu", c0 // 512)])
            E("vector", lambda v, dst=dst, n=n: v.tensor_copy(dst[:, 0:n], psu[:, 0:n]),
              reads=[("psu", 0), ("psu", 1) if n > 512 else ("psu", 0)],
              writes=[(name,)])
        em.run_block(nc)
        free(g_sp)
        free(g_rv)
        em.retire([("rv",), ("psu", 0), ("psu", 1)])

        g_a = []
        gg, xsb = man("xsb", [128, 3 * 1024], F32); g_a.append(gg)
        gg, x1t = man("x1t", [128, 3 * 1024], BF16); g_a.append(gg)
        gg, stgA = man("stgA", [128, 2 * 4096], BF16); g_a.append(gg)
        gg, ptrA0 = manp("ptrA0", [128, 128], BF16); g_a.append(gg)
        gg, ptrA1 = manp("ptrA1", [128, 128], BF16); g_a.append(gg)
        ptrA = (ptrA0, ptrA1)

        def ln_stats(src_ap, src_ids, sp):
            o = sp * 20
            sid = ("stats", sp)
            E("vector", lambda v: v.bn_stats(stats[:, o:o + 6], src_ap[:, 0:512]),
              reads=src_ids, writes=[sid])
            E("vector", lambda v: v.bn_stats(stats[:, o + 6:o + 12], src_ap[:, 512:1024]),
              reads=src_ids, writes=[sid])
            E("vector", lambda v: v.bn_aggr(stats[:, o + 12:o + 14], stats[:, o:o + 12]),
              reads=[sid], writes=[sid])
            E("vector", lambda v: v.tensor_scalar(stats[:, o + 17:o + 18],
                                                  stats[:, o + 12:o + 13],
                                                  -1.0, None, ALU.mult),
              reads=[sid], writes=[sid])
            E("scalar", lambda s: s.activation(stats[:, o + 14:o + 15],
                                               stats[:, o + 13:o + 14], AF.Ln,
                                               bias=ceps[:], scale=1.0),
              reads=[sid, ("ceps",)], writes=[sid])
            E("scalar", lambda s: s.activation(stats[:, o + 15:o + 16],
                                               stats[:, o + 14:o + 15], AF.Exp,
                                               scale=-0.5),
              reads=[sid], writes=[sid])
            E("scalar", lambda s: s.activation(stats[:, o + 16:o + 17],
                                               stats[:, o + 17:o + 18], AF.Identity,
                                               scale=stats[:, o + 15:o + 16]),
              reads=[sid], writes=[sid])
            return stats[:, o + 15:o + 16], stats[:, o + 16:o + 17], sid

        def a_stage1(tt):
            xb = tt % 3
            sp = tt % 3
            xa = xsb[:, xb * 1024:(xb + 1) * 1024]
            DMA("sync", lambda s, tt=tt, xa=xa: s.dma_start(
                xa, x_d[tt * 128:(tt + 1) * 128, :]), writes=[("xA", xb)])
            ln_stats(xa, [("xA", xb)], sp)

        def a_stage2(tt):
            xb = tt % 3
            sp = tt % 3
            o = sp * 20
            sid = ("stats", sp)
            xa = xsb[:, xb * 1024:(xb + 1) * 1024]
            x1a = x1t[:, sp * 1024:(sp + 1) * 1024]
            E("scalar", lambda s, xa=xa, x1a=x1a, sp=sp, xb=xb, o=o, sid=sid:
              s.activation(x1a, xa, AF.Identity, bias=stats[:, o + 16:o + 17],
                           scale=stats[:, o + 15:o + 16]),
              reads=[("xA", xb), sid], writes=[("x1t", sp)])

        def a_stage3(tt):
            sp = tt % 3
            sgp = (tt // 4) % 2
            x1a = x1t[:, sp * 1024:(sp + 1) * 1024]
            for dc in range(DC):
                pp = dc % 2
                E("tensor", lambda t, x1a=x1a, dc=dc, pp=pp: t.transpose(
                    ptrA[pp][:], x1a[:, dc * 128:(dc + 1) * 128], eyeb[:]),
                  reads=[("x1t", sp), ("eyeb",)], writes=[("ptrA", pp)])
                dco = sgp * 4096 + dc * 512 + (tt % 4) * 128
                if dc % 2 == 0:
                    E("vector", lambda v, pp=pp, dco=dco: v.tensor_copy(
                        stgA[:, dco:dco + 128], ptrA[pp][:]),
                      reads=[("ptrA", pp)], writes=[("stgA", sgp, dc, tt % 4)])
                else:
                    E("scalar", lambda s, pp=pp, dco=dco: s.activation(
                        stgA[:, dco:dco + 128], ptrA[pp][:], AF.Copy),
                      reads=[("ptrA", pp)], writes=[("stgA", sgp, dc, tt % 4)])
            if tt % 4 == 3:
                st = tt // 4
                for dc in range(DC):
                    dco = sgp * 4096 + dc * 512
                    DMA("sync", lambda s, dc=dc, dco=dco, st=st: s.dma_start(
                        x1fm[dc * 128:(dc + 1) * 128, st * 512:(st + 1) * 512],
                        stgA[:, dco:dco + 512]),
                        reads=[("stgA", sgp, dc, j) for j in range(4)],
                        writes=[("x1fm", dc, st)])

        for i in range(NTT + 2):
            if i < NTT:
                a_stage1(i)
            if 1 <= i <= NTT:
                a_stage2(i - 1)
            if 2 <= i:
                a_stage3(i - 2)
        em.run_block(nc)
        free(g_a)
        em.retire_prefix("xA"); em.retire_prefix("x1t"); em.retire_prefix("stgA")
        em.retire_prefix("ptrA")

        g_b = []
        gg, wqs = man("wqs", [128, 2 * 1024], BF16); g_b.append(gg)
        gg, wks = man("wks", [128, 2 * 1024], BF16); g_b.append(gg)
        gg, wvs = man("wvs", [128, 2 * 1024], BF16); g_b.append(gg)
        gg, x1c = man("x1c", [128, 2 * 4096], BF16); g_b.append(gg)
        gg, qzb = man("qzb", [128, 4 * 4096], BF16); g_b.append(gg)
        gg, ksb = man("ksb", [128, 2 * 4096], BF16); g_b.append(gg)
        gg, v65 = man("v65", [128, 2 * 4160], BF16); g_b.append(gg)
        gg, ptb = man("ptb", [128, 4 * 1024], BF16); g_b.append(gg)
        gg, onesbf = man("onesbf", [128, 64], BF16); g_b.append(gg)
        gg, srow = man("srow", [128, 2 * 512], BF16); g_b.append(gg)
        gg, rinv = man("rinv", [64, 2 * 512], F32); g_b.append(gg)
        gg, ost = man("ost", [64, 2 * 512], F32); g_b.append(gg)
        gg, pool0 = manp("pool0", [128, 1024], F32); g_b.append(gg)
        gg, pool1 = manp("pool1", [128, 1024], F32); g_b.append(gg)
        gg, pool2 = manp("pool2", [128, 1024], F32); g_b.append(gg)
        gg, pso0 = manp("pso0", [65, 512], F32); g_b.append(gg)
        gg, pso1 = manp("pso1", [65, 512], F32); g_b.append(gg)
        pool = (pool0, pool1, pool2)
        pso = (pso0, pso1)
        scnt = [0]
        pslots = {}

        def next_slot():
            sl = scnt[0] % 3
            scnt[0] += 1
            return sl

        x1fm_r = x1fm.rearrange("(c p) t -> p c t", p=128)

        E("vector", lambda v: v.memset(onesbf[:], 1.0), writes=[("onesbf",)])
        for par in range(2):
            E("vector", lambda v, par=par: v.memset(
                qzb[64:128, (par * 2) * 4096:(par * 2 + 1) * 4096], 0.0),
              writes=[("qz0z", par)])
            E("vector", lambda v, par=par: v.memset(
                qzb[0:64, (par * 2 + 1) * 4096:(par * 2 + 2) * 4096], 0.0),
              writes=[("qz1z", par)])
        for par in range(2):
            E("vector", lambda v, par=par: v.memset(
                v65[:, par * 4160 + 64:(par + 1) * 4160:65], 1.0),
              writes=[("v65o", par)])

        def load_qkvw(hp):
            par = hp % 2
            for nm, wsb, wd in (("wq", wqs, wq_d), ("wk", wks, wk_d), ("wv", wvs, wv_d)):
                DMA("gpsimd", lambda g, wsb=wsb, wd=wd, par=par, hp=hp: g.dma_start(
                    wsb[:, par * 1024:(par + 1) * 1024].rearrange("p (c n) -> p c n", c=8),
                    wd.rearrange("(c p) n -> p c n", p=128)[:, :, hp * 128:(hp + 1) * 128]),
                    writes=[(nm, par)])

        def stream_x1(hp, tc):
            xpar = tc % 2
            DMA("sync", lambda s, xpar=xpar, tc=tc: s.dma_start(
                x1c[:, xpar * 4096:(xpar + 1) * 4096].rearrange("p (c t) -> p c t", c=8),
                x1fm_r[:, :, tc * 512:(tc + 1) * 512]),
                reads=[("x1fm", dc, tc) for dc in range(DC)],
                writes=[("x1c", xpar)])
            return xpar

        def emit_V_tt(hp, tc, j):
            par = hp % 2
            xpar = tc % 2
            xco = xpar * 4096
            tt = tc * 4 + j
            sl = next_slot()
            for dc in range(DC):
                E("tensor", lambda t, xco=xco, dc=dc, j=j, par=par, sl=sl: t.matmul(
                    pool[sl][:, 0:128],
                    x1c[:, xco + dc * 512 + j * 128:xco + dc * 512 + (j + 1) * 128],
                    wvs[:, par * 1024 + dc * 128:par * 1024 + (dc + 1) * 128],
                    start=(dc == 0), stop=False),
                  reads=[("wv", par), ("x1c", xpar)],
                  writes=[("pool", sl, 0)])
            E("tensor", lambda t, hp=hp, sl=sl: t.matmul(
                pool[sl][:, 0:128], rbf[0:1, 1024:1152],
                rbf[0:1, hp * 128:(hp + 1) * 128],
                start=False, stop=True),
              reads=[("rbf",)], writes=[("pool", sl, 0)])
            for hh in range(2):
                E("vector", lambda v, par=par, tt=tt, hh=hh, sl=sl: v.tensor_copy(
                    v65[:, par * 4160 + tt * 130 + hh * 65:
                        par * 4160 + tt * 130 + hh * 65 + 64],
                    pool[sl][:, hh * 64:(hh + 1) * 64]),
                  reads=[("pool", sl, 0)], writes=[("v65", par, tt, hh)])

        def emit_QK(hp, tc):
            par = hp % 2
            xpar = tc % 2
            xco = xpar * 4096
            sl = next_slot()
            for dc in range(DC):
                E("tensor", lambda t, dc=dc, xco=xco, par=par, sl=sl:
                  t.matmul(pool[sl][:, 0:512],
                           wqs[:, par * 1024 + dc * 128:par * 1024 + (dc + 1) * 128],
                           x1c[:, xco + dc * 512:xco + (dc + 1) * 512],
                           start=(dc == 0), stop=(dc == DC - 1)),
                  reads=[("wq", par), ("x1c", xpar)],
                  writes=[("pool", sl, 0)])
            for h2 in range(2):
                E("vector", lambda v, tc=tc, par=par, sl=sl, h2=h2, hp=hp:
                  v.tensor_scalar(
                      qzb[64 * h2:64 * h2 + 64,
                          (par * 2 + h2) * 4096 + tc * 512:
                          (par * 2 + h2) * 4096 + (tc + 1) * 512],
                      pool[sl][64 * h2:64 * h2 + 64, 0:512],
                      bc[64 * h2:64 * h2 + 64, hp * 2:hp * 2 + 1], None, ALU.add),
                  reads=[("pool", sl, 0), ("bc",)],
                  writes=[("qk0", par, tc, h2)])
            for dc in range(DC):
                E("tensor", lambda t, dc=dc, xco=xco, par=par, sl=sl:
                  t.matmul(pool[sl][:, 512:1024],
                           wks[:, par * 1024 + dc * 128:par * 1024 + (dc + 1) * 128],
                           x1c[:, xco + dc * 512:xco + (dc + 1) * 512],
                           start=(dc == 0), stop=(dc == DC - 1)),
                  reads=[("wk", par), ("x1c", xpar)],
                  writes=[("pool", sl, 1)])
            E("vector", lambda v, tc=tc, par=par, sl=sl, hp=hp:
              v.tensor_scalar(ksb[:, par * 4096 + tc * 512:par * 4096 + (tc + 1) * 512],
                              pool[sl][:, 512:1024],
                              bc[:, hp * 2 + 1:hp * 2 + 2], None, ALU.add),
              reads=[("pool", sl, 1), ("bc",)],
              writes=[("qk1", par, tc)])

        load_qkvw(0)
        for tc in range(NTC):
            stream_x1(0, tc)
            emit_QK(0, tc)
            for j in range(4):
                emit_V_tt(0, tc, j)

        for hp in range(HP):
            par = hp % 2
            vtasks = []
            if hp + 1 < HP:
                load_qkvw(hp + 1)
                for tc in range(NTC):
                    vtasks.append(("x1", tc, 0))
                    vtasks.append(("qk", tc, 0))
                    for j in range(4):
                        vtasks.append(("v", tc, j))
            vti = 0

            qb = par * 4096

            def emit_S(h2, qst, u, par=par, qb=qb):
                sl = next_slot()
                pslots[(h2, u % 8)] = sl
                for half in range(2):
                    kt = 2 * u + half
                    off = max(0, kt - 4 * qst) * 128
                    qzo = (par * 2 + h2) * 4096
                    E("tensor", lambda t, h2=h2, kt=kt, half=half, qst=qst,
                      par=par, qb=qb, sl=sl, off=off, qzo=qzo:
                      t.matmul(
                        pool[sl][:, half * 512 + off:(half + 1) * 512],
                        ksb[:, qb + kt * 128:qb + (kt + 1) * 128],
                        qzb[:, qzo + qst * 512 + off:qzo + (qst + 1) * 512],
                        start=True, stop=True),
                      reads=[("qk1", par, kt // 4), ("qk0", par, qst, h2),
                             ("qz0z", par), ("qz1z", par)],
                      writes=[("pool", sl, half)])

            def emit_mask_exp(h2, qst, u):
                sl = pslots[(h2, u % 8)]
                for half in range(2):
                    j = 2 * u + half - 4 * qst
                    if j >= 0:
                        off = 128 * j
                        E("vector", lambda v, sl=sl, half=half, j=j, off=off: v.tensor_add(
                            pool[sl][:, half * 512 + off:half * 512 + off + 128],
                            pool[sl][:, half * 512 + off:half * 512 + off + 128],
                            msk[:, j * 128:(j + 1) * 128]),
                          reads=[("msk",), ("pool", sl, half)],
                          writes=[("pool", sl, half)])
                pslot = h2 * 2 + u % 2
                E("scalar", lambda s, sl=sl, pslot=pslot: s.activation(
                    ptb[:, pslot * 1024:(pslot + 1) * 1024], pool[sl][:],
                    AF.Exp, bias=cm16[:], scale=0.125),
                  reads=[("pool", sl, 0), ("pool", sl, 1), ("cm16",)],
                  writes=[("ptb", pslot)])

            def emit_PV(h2, qst, u, opar, npair, par=par):
                pslot = h2 * 2 + u % 2
                for half in range(2):
                    kt = 2 * u + half
                    off = max(0, kt - 4 * qst) * 128
                    E("tensor", lambda t, h2=h2, kt=kt, half=half, opar=opar,
                      npair=npair, par=par, pslot=pslot, off=off: t.matmul(
                        pso[opar][:, off:512],
                        v65[:, par * 4160 + kt * 130 + h2 * 65:
                            par * 4160 + kt * 130 + h2 * 65 + 65],
                        ptb[:, pslot * 1024 + half * 512 + off:
                            pslot * 1024 + (half + 1) * 512],
                        start=(kt == 0), stop=(kt == 2 * npair - 1)),
                      reads=[("v65", par, kt, h2), ("v65o", par), ("ptb", pslot)],
                      writes=[("pso", opar)])

            def emit_norm(h2, qst, opar, hp=hp):
                rp = opar % 2
                E("vector", lambda v, opar=opar, rp=rp: v.tensor_copy(
                    srow[64:65, rp * 512:(rp + 1) * 512], pso[opar][64:65, :]),
                  reads=[("pso", opar)], writes=[("srow", rp)])
                E("vector", lambda v, opar=opar: v.tensor_copy(
                    ost[:, (opar % 2) * 512:(opar % 2 + 1) * 512],
                    pso[opar][0:64, :]),
                  reads=[("pso", opar)], writes=[("ost", opar % 2)])
                bsl = next_slot()
                E("tensor", lambda t, rp=rp, bsl=bsl: t.matmul(
                    pool[bsl][0:64, 0:512], onesbf[64:65, 0:64],
                    srow[64:65, rp * 512:(rp + 1) * 512],
                    start=True, stop=True),
                  reads=[("srow", rp), ("onesbf",)], writes=[("pool", bsl, 0)])
                E("scalar", lambda s, rp=rp, bsl=bsl: s.activation(
                    rinv[:, rp * 512:(rp + 1) * 512], pool[bsl][0:64, 0:512], AF.Ln),
                  reads=[("pool", bsl, 0)], writes=[("rinv", rp)])
                E("scalar", lambda s, rp=rp: s.activation(
                    rinv[:, rp * 512:(rp + 1) * 512],
                    rinv[:, rp * 512:(rp + 1) * 512], AF.Exp, scale=-1.0),
                  reads=[("rinv", rp)], writes=[("rinv", rp)])
                E("vector", lambda v, h2=h2, qst=qst, opar=opar, hp=hp, rp=rp:
                  v.tensor_mul(
                    cmb[64 * h2:64 * h2 + 64,
                        hp * 4096 + qst * 512:hp * 4096 + (qst + 1) * 512],
                    ost[:, rp * 512:(rp + 1) * 512],
                    rinv[:, rp * 512:(rp + 1) * 512]),
                  reads=[("ost", rp), ("rinv", rp)],
                  writes=[("cmb", hp, h2, qst)])

            rounds_total = sum(2 * qst + 2 for qst in range(NTC))
            ridx = 0
            uidx = 0
            for qst in range(NTC):
                npair = 2 * qst + 2
                op0 = uidx % 2; uidx += 1
                op1 = uidx % 2; uidx += 1
                emit_S(0, qst, 0)
                emit_S(1, qst, 0)
                for u in range(npair):
                    emit_mask_exp(0, qst, u)
                    emit_mask_exp(1, qst, u)
                    if u + 1 < npair:
                        emit_S(0, qst, u + 1)
                        emit_S(1, qst, u + 1)
                    emit_PV(0, qst, u, op0, npair)
                    emit_PV(1, qst, u, op1, npair)
                    ridx += 1
                    want = (ridx * len(vtasks)) // rounds_total if vtasks else 0
                    while vti < want:
                        kind, tc, j = vtasks[vti]
                        if kind == "x1":
                            stream_x1(hp + 1, tc)
                        elif kind == "qk":
                            emit_QK(hp + 1, tc)
                        else:
                            emit_V_tt(hp + 1, tc, j)
                        vti += 1
                emit_norm(0, qst, op0)
                emit_norm(1, qst, op1)
            while vti < len(vtasks):
                kind, tc, j = vtasks[vti]
                if kind == "x1":
                    stream_x1(hp + 1, tc)
                elif kind == "qk":
                    emit_QK(hp + 1, tc)
                else:
                    emit_V_tt(hp + 1, tc, j)
                vti += 1
        with nc.allow_low_precision("bf16 softmax denominators, within bf16-matmul tolerance"):
            em.run_block(nc)
        free(g_b)
        for pre in ("wq", "wk", "wv", "x1c", "qk0", "qk1", "v65", "v65o", "ptb",
                    "rinv", "srow", "ost", "pool", "pso", "onesbf"):
            em.retire_prefix(pre)

        g_c = []
        gg, wos = man("wos", [128, 8 * 1024], BF16); g_c.append(gg)
        gg, xsc = man("xsc", [128, 3 * 1024], F32); g_c.append(gg)
        gg, awc = man("awc", [128, 6 * 1024], F32); g_c.append(gg)
        gg, xhb = man("xhb", [128, 3 * 1024], BF16); g_c.append(gg)
        gg, xlb = man("xlb", [128, 3 * 1024], BF16); g_c.append(gg)
        gg, stg2 = man("stg2", [128, 2 * 4096], BF16); g_c.append(gg)
        gg, psc0 = manp("psc0", [128, 1024], F32); g_c.append(gg)
        gg, psc1 = manp("psc1", [128, 1024], F32); g_c.append(gg)
        gg, psc2 = manp("psc2", [128, 1024], F32); g_c.append(gg)
        psc = (psc0, psc1, psc2)

        DMA("gpsimd", lambda g: g.dma_start(
            wos[:].rearrange("p (c n) -> p c n", c=8),
            wo_d.rearrange("(c p) n -> p c n", p=128)), writes=[("wos",)])

        def c_stage1(tt):
            sp = tt % 3
            xa = xsc[:, sp * 1024:(sp + 1) * 1024]
            ao = awc[:, sp * 1024:(sp + 1) * 1024]
            DMA("sync", lambda s, xa=xa, tt=tt: s.dma_start(
                xa, x_d[tt * 128:(tt + 1) * 128, :]), writes=[("xC", sp)])
            for of2 in range(2):
                for cfc in range(DC):
                    E("tensor", lambda t, sp=sp, of2=of2, cfc=cfc, tt=tt: t.matmul(
                        psc[sp][:, of2 * 512:(of2 + 1) * 512],
                        cmb[:, cfc * 4096 + tt * 128:cfc * 4096 + (tt + 1) * 128],
                        wos[:, cfc * 1024 + of2 * 512:cfc * 1024 + (of2 + 1) * 512],
                        start=(cfc == 0), stop=(cfc == DC - 1)),
                      reads=[("cmb", cfc, 0, tt // 4), ("cmb", cfc, 1, tt // 4), ("wos",)],
                      writes=[("psc", sp, of2)])
            E("vector", lambda v, ao=ao, sp=sp: v.tensor_add(ao, psc[sp][:], obc[:]),
              reads=[("psc", sp, 0), ("psc", sp, 1), ("obc",)], writes=[("aoC", sp)])
            ln_stats(ao, [("aoC", sp)], sp)

        def c_stage2(tt):
            sp = tt % 3
            o = sp * 20
            sid = ("stats", sp)
            xa = xsc[:, sp * 1024:(sp + 1) * 1024]
            ao = awc[:, sp * 1024:(sp + 1) * 1024]
            tw = awc[:, (3 + sp) * 1024:(4 + sp) * 1024]
            rstd = stats[:, o + 15:o + 16]
            negmr = stats[:, o + 16:o + 17]
            E("scalar", lambda s, tw=tw, ao=ao, rstd=rstd, negmr=negmr: s.activation(
                tw, ao, AF.Identity, bias=negmr, scale=rstd),
              reads=[("aoC", sp), sid], writes=[("twC", sp)])
            E("vector", lambda v, ao=ao, xa=xa, tw=tw: v.tensor_add(ao, xa, tw),
              reads=[("xC", sp), ("twC", sp), ("aoC", sp)], writes=[("aoC", sp)])
            DMA("sync", lambda s, ao=ao, tt=tt: s.dma_start(
                xp_d[tt * 128:(tt + 1) * 128, :], ao), reads=[("aoC", sp)],
                writes=[("xp", tt)])
            ln_stats(ao, [("aoC", sp)], sp)

        def c_stage3a(tt):
            sp = tt % 3
            o = sp * 20
            sid = ("stats", sp)
            ao = awc[:, sp * 1024:(sp + 1) * 1024]
            tw = awc[:, (3 + sp) * 1024:(4 + sp) * 1024]
            rstd = stats[:, o + 15:o + 16]
            negmr = stats[:, o + 16:o + 17]
            E("scalar", lambda s, tw=tw, ao=ao, rstd=rstd, negmr=negmr, sp=sp, sid=sid: s.activation(
                tw, ao, AF.Identity, bias=negmr, scale=rstd),
              reads=[("aoC", sp), sid], writes=[("twC", sp)])
            xh = xhb[:, sp * 1024:(sp + 1) * 1024]
            xl = xlb[:, sp * 1024:(sp + 1) * 1024]
            E("scalar", lambda s, xh=xh, tw=tw: s.activation(xh, tw, AF.Copy),
              reads=[("twC", sp)], writes=[("xhC", sp)])
            E("vector", lambda v, xl=xl, tw=tw, xh=xh: v.tensor_sub(xl, tw, xh),
              reads=[("twC", sp), ("xhC", sp)], writes=[("xlC", sp)])

        def c_stage3b(tt):
            sp = tt % 3
            xh = xhb[:, sp * 1024:(sp + 1) * 1024]
            xl = xlb[:, sp * 1024:(sp + 1) * 1024]
            DMA("sync", lambda s, xh=xh, tt=tt: s.dma_start(
                x2fm[tt * 128:(tt + 1) * 128, :], xh),
                reads=[("xhC", sp)], writes=[("x2fm", tt)])
            DMA("sync", lambda s, xl=xl, tt=tt: s.dma_start(
                x2lo[tt * 128:(tt + 1) * 128, :], xl),
                reads=[("xlC", sp)], writes=[("x2lo", tt)])
            if tt % 4 == 3:
                tc = tt // 4
                spar = tc % 2
                rdep = [("x2fm", 4 * tc + k) for k in range(4)]
                for dc in range(DC):
                    DMA("sync", lambda s, tc=tc, dc=dc, spar=spar: s.dma_start(
                        stg2[:, spar * 4096 + dc * 512:spar * 4096 + (dc + 1) * 512],
                        x2fm[tc * 512:(tc + 1) * 512, dc * 128:(dc + 1) * 128],
                        transpose=True),
                        reads=rdep, writes=[("stg2", spar, dc)])
                    DMA("sync", lambda s, tc=tc, dc=dc, spar=spar: s.dma_start(
                        xfmd[dc * 128:(dc + 1) * 128, tc * 512:(tc + 1) * 512],
                        stg2[:, spar * 4096 + dc * 512:spar * 4096 + (dc + 1) * 512]),
                        reads=[("stg2", spar, dc)], writes=[("xfmd", dc, tc)])

        for i in range(NTT + 3):
            if i < NTT:
                c_stage1(i)
            if 1 <= i <= NTT:
                c_stage2(i - 1)
            if 2 <= i <= NTT + 1:
                c_stage3a(i - 2)
            if 3 <= i:
                c_stage3b(i - 3)
        em.run_block(nc)
        free(g_c)
        free(g_bc)
        for pre in ("wos", "xC", "aoC", "twC", "xqC", "xhC", "xlC", "stgC", "psc",
                    "ptc", "msk", "rbf", "obc", "cmb"):
            em.retire_prefix(pre)

        g_d0 = []
        gg, x2ch = man("x2ch", [128, 2 * 4096], BF16); g_d0.append(gg)
        gg, x2cl = man("x2cl", [128, 2 * 4096], BF16); g_d0.append(gg)
        gg, gwk = man("gwk", [128, 2 * 384], F32); g_d0.append(gg)
        gg, psg0 = manp("psg0", [128, 64], F32); g_d0.append(gg)
        gg, psg1 = manp("psg1", [128, 64], F32); g_d0.append(gg)
        gg, psct0 = manp("psct0", [16, 128], F32); g_d0.append(gg)
        gg, psct1 = manp("psct1", [16, 128], F32); g_d0.append(gg)
        psg = (psg0, psg1)
        psct = (psct0, psct1)

        def emit_gate_chain(tc, gp):
            g0 = gwk[:, gp * 384:(gp + 1) * 384]
            b3 = lambda cols, k: g0[:, cols[0]:cols[1]].rearrange(
                "p (a b) -> p a b", b=k)
            eg = (0, 64); orig = (64, 128); sbt = (128, 192); sbm = (192, 256)
            m8 = (256, 288); n8 = (288, 320)
            t4a = (320, 336); t4b = (336, 352); t4c = (352, 368)
            nm = (368, 372); ssum = (372, 376); rs = (376, 380); thr = (380, 384)
            psgr = [("psg", gp, j) for j in range(4)]

            def C2(c):
                return g0[:, c[0]:c[1]]

            def BCL(c, n, k):
                return C2(c).rearrange("p (a b) -> p a b", b=1).broadcast_to([128, n, k])

            def V(fn, r=()):
                E("vector", fn, reads=[gid] + list(r), writes=[gid])

            V(lambda v: v.reduce_max(b3(nm, 1), psg[gp][:].rearrange(
                "p (a b) -> p a b", b=16), AX.X, negate=True), r=psgr)
            for j in range(4):
                E("scalar", lambda s, j=j: s.activation(
                    C2(eg)[:, j * 16:(j + 1) * 16], psg[gp][:, j * 16:(j + 1) * 16],
                    AF.Exp, bias=C2(nm)[:, j:j + 1], accum_out=C2(ssum)[:, j:j + 1]),
                  reads=[gid] + psgr, writes=[gid])
            V(lambda v: v.reciprocal(C2(rs), C2(ssum)))
            V(lambda v: v.tensor_mul(b3(eg, 16), b3(eg, 16), BCL(rs, 4, 16)))
            V(lambda v: v.tensor_copy(C2(orig), C2(eg)))
            V(lambda v: v.tensor_add(b3(sbt, 16), b3(orig, 16),
                                     gbc[:, 0:16].rearrange("p (a b) -> p a b", a=1)
                                     .broadcast_to([128, 4, 16])), r=[("gbc",)])
            V(lambda v: v.tensor_tensor(b3(m8, 8), b3(sbt, 16)[:, :, 0::2],
                                        b3(sbt, 16)[:, :, 1::2], ALU.max))
            V(lambda v: v.tensor_tensor(b3(n8, 8), b3(sbt, 16)[:, :, 0::2],
                                        b3(sbt, 16)[:, :, 1::2], ALU.min))
            V(lambda v: v.tensor_add(b3(t4a, 4), b3(m8, 8)[:, :, 0::2],
                                     b3(m8, 8)[:, :, 1::2]))
            V(lambda v: v.tensor_add(b3(t4b, 4), b3(m8, 8)[:, :, 0::2],
                                     b3(n8, 8)[:, :, 0::2]))
            V(lambda v: v.tensor_add(b3(t4c, 4), b3(m8, 8)[:, :, 1::2],
                                     b3(n8, 8)[:, :, 1::2]))
            V(lambda v: v.tensor_tensor(C2(t4a), C2(t4a), C2(t4b), ALU.max))
            V(lambda v: v.tensor_tensor(C2(t4a), C2(t4a), C2(t4c), ALU.max))
            V(lambda v: v.tensor_tensor(b3(t4b, 4)[:, :, 0:2] if False else
                                        g0[:, 336:344].rearrange("p (a b) -> p a b", b=2),
                                        b3(t4a, 4)[:, :, 0::2],
                                        b3(t4a, 4)[:, :, 1::2], ALU.max))
            V(lambda v: v.tensor_tensor(g0[:, 344:352].rearrange("p (a b) -> p a b", b=2),
                                        b3(t4a, 4)[:, :, 0::2],
                                        b3(t4a, 4)[:, :, 1::2], ALU.min))
            V(lambda v: v.tensor_tensor(g0[:, 352:356].rearrange("p (a b) -> p a b", b=1),
                                        g0[:, 336:344].rearrange("p (a b) -> p a b", b=2)[:, :, 0:1],
                                        g0[:, 336:344].rearrange("p (a b) -> p a b", b=2)[:, :, 1:2],
                                        ALU.min))
            V(lambda v: v.tensor_tensor(g0[:, 356:360].rearrange("p (a b) -> p a b", b=1),
                                        g0[:, 344:352].rearrange("p (a b) -> p a b", b=2)[:, :, 0:1],
                                        g0[:, 344:352].rearrange("p (a b) -> p a b", b=2)[:, :, 1:2],
                                        ALU.max))
            V(lambda v: v.tensor_tensor(C2(thr), g0[:, 352:356], g0[:, 356:360], ALU.max))
            V(lambda v: v.tensor_tensor(b3(t4b, 4), b3(t4a, 4), BCL(thr, 4, 4), ALU.is_ge))
            V(lambda v: v.tensor_scalar(C2(t4b), C2(t4b), -1.0, 1e30, ALU.add, ALU.mult))
            V(lambda v: v.tensor_add(
                g0[:, sbm[0]:sbm[1]].rearrange("p (a g e) -> p a g e", a=4, g=4),
                g0[:, sbt[0]:sbt[1]].rearrange("p (a g e) -> p a g e", a=4, g=4),
                g0[:, t4b[0]:t4b[1]].rearrange("p (a g b) -> p a g b", a=4, b=1)
                .broadcast_to([128, 4, 4, 4])))
            V(lambda v: v.tensor_tensor(b3(m8, 8), b3(sbm, 16)[:, :, 0::2],
                                        b3(sbm, 16)[:, :, 1::2], ALU.max))
            V(lambda v: v.tensor_tensor(b3(n8, 8), b3(sbm, 16)[:, :, 0::2],
                                        b3(sbm, 16)[:, :, 1::2], ALU.min))
            V(lambda v: v.tensor_tensor(b3(t4a, 4), b3(m8, 8)[:, :, 0::2],
                                        b3(m8, 8)[:, :, 1::2], ALU.max))
            V(lambda v: v.tensor_tensor(b3(t4b, 4), b3(m8, 8)[:, :, 0::2],
                                        b3(m8, 8)[:, :, 1::2], ALU.min))
            V(lambda v: v.tensor_tensor(b3(t4c, 4), b3(n8, 8)[:, :, 0::2],
                                        b3(n8, 8)[:, :, 1::2], ALU.max))
            V(lambda v: v.tensor_tensor(C2(t4b), C2(t4b), C2(t4c), ALU.max))
            V(lambda v: v.tensor_tensor(g0[:, 352:360].rearrange("p (a b) -> p a b", b=2),
                                        b3(t4a, 4)[:, :, 0::2],
                                        b3(t4a, 4)[:, :, 1::2], ALU.max))
            V(lambda v: v.tensor_tensor(g0[:, 360:368].rearrange("p (a b) -> p a b", b=2),
                                        b3(t4a, 4)[:, :, 0::2],
                                        b3(t4a, 4)[:, :, 1::2], ALU.min))
            V(lambda v: v.tensor_tensor(g0[:, 320:328].rearrange("p (a b) -> p a b", b=2),
                                        b3(t4b, 4)[:, :, 0::2],
                                        b3(t4b, 4)[:, :, 1::2], ALU.max))
            V(lambda v: v.tensor_tensor(g0[:, 320:328].rearrange("p (a b) -> p a b", b=2),
                                        g0[:, 320:328].rearrange("p (a b) -> p a b", b=2),
                                        g0[:, 360:368].rearrange("p (a b) -> p a b", b=2),
                                        ALU.max))
            V(lambda v: v.tensor_tensor(g0[:, 328:332].rearrange("p (a b) -> p a b", b=1),
                                        g0[:, 352:360].rearrange("p (a b) -> p a b", b=2)[:, :, 0:1],
                                        g0[:, 352:360].rearrange("p (a b) -> p a b", b=2)[:, :, 1:2],
                                        ALU.min))
            V(lambda v: v.tensor_tensor(g0[:, 332:336].rearrange("p (a b) -> p a b", b=1),
                                        g0[:, 320:328].rearrange("p (a b) -> p a b", b=2)[:, :, 0:1],
                                        g0[:, 320:328].rearrange("p (a b) -> p a b", b=2)[:, :, 1:2],
                                        ALU.max))
            V(lambda v: v.tensor_tensor(C2(thr), g0[:, 328:332], g0[:, 332:336], ALU.max))
            V(lambda v: v.tensor_tensor(b3(sbm, 16), b3(sbm, 16), BCL(thr, 4, 16),
                                        ALU.is_ge))
            E("vector", lambda v, tc=tc: v.tensor_mul(
                ctm[:, tc * 64:(tc + 1) * 64], C2(orig), C2(sbm)),
              reads=[gid], writes=[("ctm", tc)])

        for tc in range(NTC):
            xpar = tc % 2
            rdepl = [("x2lo", 4 * tc + k) for k in range(4)]
            xfmd_r = xfmd.rearrange("(c p) t -> p c t", p=128)
            DMA("sync", lambda s, xpar=xpar, tc=tc, xfmd_r=xfmd_r: s.dma_start(
                x2ch[:, xpar * 4096:(xpar + 1) * 4096].rearrange("p (c t) -> p c t", c=8),
                xfmd_r[:, :, tc * 512:(tc + 1) * 512]),
                reads=[("xfmd", dc, tc) for dc in range(DC)],
                writes=[("x2ch", xpar, dc) for dc in range(DC)])
            for dc in range(DC):
                DMA("sync", lambda s, xpar=xpar, tc=tc, dc=dc: s.dma_start(
                    x2cl[:, xpar * 4096 + dc * 512:xpar * 4096 + (dc + 1) * 512],
                    x2lo[tc * 512:(tc + 1) * 512, dc * 128:(dc + 1) * 128],
                    transpose=True),
                    reads=rdepl, writes=[("x2cl", xpar, dc)])
            gp = tc % 2
            gid = ("gwk", gp)
            for j in range(4):
                idx = 0
                for xsrc, xnm, grhs, gid_ in ((x2ch, "x2ch", gwhi, ("gwhi",)),
                                              (x2cl, "x2cl", gwhi, ("gwhi",)),
                                              (x2ch, "x2ch", gwlo, ("gwlo",))):
                    for dc in range(DC):
                        E("tensor", lambda t, xsrc=xsrc, xpar=xpar, dc=dc, j=j,
                          grhs=grhs, gp=gp, idx=idx: t.matmul(
                            psg[gp][:, j * 16:(j + 1) * 16],
                            xsrc[:, xpar * 4096 + dc * 512 + j * 128:
                                 xpar * 4096 + dc * 512 + (j + 1) * 128],
                            grhs[:, dc * 16:(dc + 1) * 16],
                            start=(idx == 0), stop=(idx == 23)),
                          reads=[(xnm, xpar, dc), gid_], writes=[("psg", gp, j)])
                        idx += 1
            emit_gate_chain(tc, gp)
            for j in range(4):
                tt = tc * 4 + j
                E("tensor", lambda t, gp=gp, tc=tc, j=j: t.transpose(
                    psct[gp][:], ctm[:, tc * 64 + j * 16:tc * 64 + (j + 1) * 16],
                    eyef[:]),
                  reads=[("ctm", tc), ("eyef",)], writes=[("psct", gp)])
                E("scalar", lambda s, gp=gp, tt=tt: s.activation(
                    ctT[:, tt * 128:(tt + 1) * 128], psct[gp][:], AF.Copy),
                  reads=[("psct", gp)], writes=[("ctT", tt)])
        em.run_block(nc)
        free(g_d0)
        for pre in ("x2ch", "x2cl", "gwk", "psg", "psct"):
            em.retire_prefix(pre)

        g_d = []
        gg, ysb = man("ysb", [128, 16 * 1024], F32); g_d.append(gg)
        gg, x2s = man("x2s", [128, 2 * 4096], BF16); g_d.append(gg)
        gg, hsh = man("hsh", [128, 8 * 512], BF16); g_d.append(gg)
        gg, slb = man("slb", [128, 2 * 512], F32); g_d.append(gg)
        gg, hxb = man("hxb", [128, 2 * 2048], BF16); g_d.append(gg)
        gg, yos = man("yos", [128, 2 * 1024], F32); g_d.append(gg)
        gg, ewk = man("ewk", [128, 6 * 1024], F32); g_d.append(gg)
        gg, psh1 = manp("psh1", [128, 512], F32); g_d.append(gg)
        gg, psh3 = manp("psh3", [128, 512], F32); g_d.append(gg)
        gg, psy0 = manp("psy0", [128, 1024], F32); g_d.append(gg)
        gg, psy1 = manp("psy1", [128, 1024], F32); g_d.append(gg)
        psy = (psy0, psy1)
        x2cnt = [0]

        xfmd_r = xfmd.rearrange("(c p) t -> p c t", p=128)

        def stream_x2(tcg):
            par = x2cnt[0] % 2
            x2cnt[0] += 1
            DMA("sync", lambda s, par=par, tcg=tcg: s.dma_start(
                x2s[:, par * 4096:(par + 1) * 4096].rearrange("p (c t) -> p c t", c=8),
                xfmd_r[:, :, tcg * 512:(tcg + 1) * 512]),
                reads=[("xfmd", dc, tcg) for dc in range(DC)], writes=[("x2s", par)])
            return par

        for half in range(2):
            g_ws = []
            gg, ws1s = man(f"ws1s{half}", [128, 8 * 1024], BF16); g_ws.append(gg)
            gg, ws3s = man(f"ws3s{half}", [128, 8 * 1024], BF16); g_ws.append(gg)
            gg, ws2s = man(f"ws2s{half}", [128, 8 * 1024], BF16); g_ws.append(gg)
            for nm, wsb, wd in (("ws1", ws1s, ws1_d), ("ws3", ws3s, ws3_d),
                                ("ws2", ws2s, ws2_d)):
                DMA("gpsimd", lambda g, wsb=wsb, wd=wd: g.dma_start(
                    wsb[:].rearrange("p (c n) -> p c n", c=8),
                    wd.rearrange("(c p) n -> p c n", p=128)),
                    writes=[(nm, half)])
            for tc4 in range(4):
                tcg = half * 4 + tc4
                xpar = stream_x2(tcg)
                xo = xpar * 4096
                for ifc in range(8):
                    E_h1 = [("ws1", half), ("x2s", xpar)]
                    for dc in range(DC):
                        E("tensor", lambda t, ifc=ifc, dc=dc, xo=xo: t.matmul(
                            psh1[:], ws1s[:, dc * 1024 + ifc * 128:dc * 1024 + (ifc + 1) * 128],
                            x2s[:, xo + dc * 512:xo + (dc + 1) * 512],
                            start=(dc == 0), stop=(dc == DC - 1)),
                          reads=E_h1, writes=[("psh1",)])
                    sp = ifc % 2
                    E("scalar", lambda s, sp=sp, ifc=ifc: s.activation(
                        slb[:, sp * 512:(sp + 1) * 512], psh1[:], AF.Silu,
                        bias=bc[:, 144 + ifc:145 + ifc]),
                      reads=[("psh1",), ("bc",)], writes=[("slb", sp)])
                    for dc in range(DC):
                        E("tensor", lambda t, ifc=ifc, dc=dc, xo=xo: t.matmul(
                            psh3[:], ws3s[:, dc * 1024 + ifc * 128:dc * 1024 + (ifc + 1) * 128],
                            x2s[:, xo + dc * 512:xo + (dc + 1) * 512],
                            start=(dc == 0), stop=(dc == DC - 1)),
                          reads=[("ws3", half), ("x2s", xpar)], writes=[("psh3",)])
                    E("vector", lambda v, ifc=ifc, sp=sp: v.scalar_tensor_tensor(
                        hsh[:, ifc * 512:(ifc + 1) * 512], psh3[:],
                        bc[:, 152 + ifc:153 + ifc], slb[:, sp * 512:(sp + 1) * 512],
                        ALU.add, ALU.mult),
                      reads=[("psh3",), ("bc",), ("slb", sp)], writes=[("hsh", ifc)])
                for j in range(4):
                    ttl = tc4 * 4 + j
                    tt = half * 16 + ttl
                    jp = j % 2
                    for of2 in range(2):
                        for ifc in range(8):
                            E("tensor", lambda t, jp=jp, of2=of2, ifc=ifc, j=j: t.matmul(
                                psy[jp][:, of2 * 512:(of2 + 1) * 512],
                                hsh[:, ifc * 512 + j * 128:ifc * 512 + (j + 1) * 128],
                                ws2s[:, ifc * 1024 + of2 * 512:ifc * 1024 + (of2 + 1) * 512],
                                start=(ifc == 0), stop=False),
                              reads=[("hsh", ifc), ("ws2", half)],
                              writes=[("psy", jp, of2)])
                        E("tensor", lambda t, jp=jp, of2=of2, tt=tt: t.matmul(
                            psy[jp][:, of2 * 512:(of2 + 1) * 512],
                            ctT[:, tt * 128:(tt + 1) * 128],
                            be2b[:, of2 * 512:(of2 + 1) * 512],
                            start=False, stop=True),
                          reads=[("ctT", tt), ("be2b",)], writes=[("psy", jp, of2)])
                    E("vector", lambda v, ttl=ttl, jp=jp: v.tensor_add(
                        ysb[:, ttl * 1024:(ttl + 1) * 1024], psy[jp][:], sbc[:]),
                      reads=[("psy", jp, 0), ("psy", jp, 1), ("sbc",)],
                      writes=[("y", ttl)])
            em.run_block(nc)
            free(g_ws)
            em.retire([("ws1", half), ("ws3", half), ("ws2", half)])

            g_we = []
            gg, we1s = man(f"we1s{half}", [128, 2 * 4096], BF16); g_we.append(gg)
            gg, we3s = man(f"we3s{half}", [128, 2 * 4096], BF16); g_we.append(gg)
            gg, we2s = man(f"we2s{half}", [128, 2 * 4096], BF16); g_we.append(gg)

            def load_expert(e):
                wp = e % 2
                for nm, wsb, wd in (("we1", we1s, we1_d), ("we3", we3s, we3_d)):
                    DMA("gpsimd", lambda g, wsb=wsb, wd=wd, wp=wp, e=e: g.dma_start(
                        wsb[:, wp * 4096:(wp + 1) * 4096].rearrange("p (c n) -> p c n", c=8),
                        wd[e * D:(e + 1) * D, :].rearrange("(c p) n -> p c n", p=128)),
                        writes=[(nm, wp)])
                DMA("gpsimd", lambda g, wp=wp, e=e: g.dma_start(
                    we2s[:, wp * 4096:(wp + 1) * 4096].rearrange("p (c n) -> p c n", c=4),
                    we2_d[e * IN:(e + 1) * IN, :].rearrange("(c p) n -> p c n", p=128)),
                    writes=[("we2", wp)])

            load_expert(0)
            load_expert(1)
            for e in range(NE):
                wp = e % 2
                wo_ = wp * 4096
                for tc4 in range(4):
                    tcg = half * 4 + tc4
                    xpar = stream_x2(tcg)
                    xo = xpar * 4096
                    for ifc in range(4):
                        for dc in range(DC):
                            E("tensor", lambda t, ifc=ifc, dc=dc, xo=xo, wo_=wo_: t.matmul(
                                psh1[:],
                                we1s[:, wo_ + dc * 512 + ifc * 128:wo_ + dc * 512 + (ifc + 1) * 128],
                                x2s[:, xo + dc * 512:xo + (dc + 1) * 512],
                                start=(dc == 0), stop=(dc == DC - 1)),
                              reads=[("we1", wp), ("x2s", xpar)], writes=[("psh1",)])
                        sp = ifc % 2
                        E("scalar", lambda s, sp=sp, e=e, ifc=ifc: s.activation(
                            slb[:, sp * 512:(sp + 1) * 512], psh1[:], AF.Silu,
                            bias=bc[:, 16 + e * 4 + ifc:17 + e * 4 + ifc]),
                          reads=[("psh1",), ("bc",)], writes=[("slb", sp)])
                        for dc in range(DC):
                            E("tensor", lambda t, ifc=ifc, dc=dc, xo=xo, wo_=wo_: t.matmul(
                                psh3[:],
                                we3s[:, wo_ + dc * 512 + ifc * 128:wo_ + dc * 512 + (ifc + 1) * 128],
                                x2s[:, xo + dc * 512:xo + (dc + 1) * 512],
                                start=(dc == 0), stop=(dc == DC - 1)),
                              reads=[("we3", wp), ("x2s", xpar)], writes=[("psh3",)])
                        hxp = tc4 % 2
                        E("vector", lambda v, ifc=ifc, sp=sp, e=e, hxp=hxp: v.scalar_tensor_tensor(
                            hxb[:, hxp * 2048 + ifc * 512:hxp * 2048 + (ifc + 1) * 512],
                            psh3[:], bc[:, 80 + e * 4 + ifc:81 + e * 4 + ifc],
                            slb[:, sp * 512:(sp + 1) * 512], ALU.add, ALU.mult),
                          reads=[("psh3",), ("bc",), ("slb", sp)],
                          writes=[("hx", hxp, ifc)])
                    for j in range(4):
                        ttl = tc4 * 4 + j
                        tt = half * 16 + ttl
                        jp = j % 2
                        for of2 in range(2):
                            for ifc in range(4):
                                hxp = tc4 % 2
                                E("tensor", lambda t, jp=jp, of2=of2, ifc=ifc, j=j,
                                  wo_=wo_, hxp=hxp: t.matmul(
                                    psy[jp][:, of2 * 512:(of2 + 1) * 512],
                                    hxb[:, hxp * 2048 + ifc * 512 + j * 128:
                                        hxp * 2048 + ifc * 512 + (j + 1) * 128],
                                    we2s[:, wo_ + ifc * 1024 + of2 * 512:
                                         wo_ + ifc * 1024 + (of2 + 1) * 512],
                                    start=(ifc == 0), stop=(ifc == 3)),
                                  reads=[("hx", hxp, ifc), ("we2", wp)],
                                  writes=[("psy", jp, of2)])
                        E("scalar", lambda s, jp=jp, tt=tt, e=e: s.activation(
                            yos[:, jp * 1024:(jp + 1) * 1024], psy[jp][:],
                            AF.Identity, scale=ctm[:, tt * 16 + e:tt * 16 + e + 1]),
                          reads=[("psy", jp, 0), ("psy", jp, 1), ("ctm", tt // 4)],
                          writes=[("yos", jp)])
                        E("vector", lambda v, ttl=ttl, jp=jp: v.tensor_add(
                            ysb[:, ttl * 1024:(ttl + 1) * 1024],
                            ysb[:, ttl * 1024:(ttl + 1) * 1024],
                            yos[:, jp * 1024:(jp + 1) * 1024]),
                          reads=[("yos", jp), ("y", ttl)], writes=[("y", ttl)])
                if e + 2 < NE:
                    load_expert(e + 2)
            if DBG:
                DMA("sync", lambda s, half=half: s.dma_start(
                    dbgy[half * 128:(half + 1) * 128, :], ysb[:]),
                    reads=[("y", t_) for t_ in range(16)])
            for ttl in range(16):
                tt = half * 16 + ttl
                sp = ttl % 2
                y2 = ewk[:, sp * 1024:(sp + 1) * 1024]
                xpl = ewk[:, (2 + sp) * 1024:(3 + sp) * 1024]
                ob = ewk[:, (4 + sp) * 1024:(5 + sp) * 1024]
                ysl = ysb[:, ttl * 1024:(ttl + 1) * 1024]
                DMA("sync", lambda s, xpl=xpl, tt=tt: s.dma_start(
                    xpl, xp_d[tt * 128:(tt + 1) * 128, :]),
                    reads=[("xp", tt)], writes=[("xpl", sp)])
                rstd, negmr, sid = ln_stats(ysl, [("y", ttl)], sp)
                E("scalar", lambda s, y2=y2, ysl=ysl, rstd=rstd, negmr=negmr: s.activation(
                    y2, ysl, AF.Identity, bias=negmr, scale=rstd),
                  reads=[("y", ttl), sid], writes=[("y2", sp)])
                E("vector", lambda v, ob=ob, xpl=xpl, y2=y2: v.tensor_add(ob, xpl, y2),
                  reads=[("xpl", sp), ("y2", sp)], writes=[("ob", sp)])
                DMA("sync", lambda s, ob=ob, tt=tt: s.dma_start(
                    out_d[tt * 128:(tt + 1) * 128, :], ob),
                    reads=[("ob", sp)], writes=[("outd", tt)])
            if half == 1:
                em.final_waits()
            em.run_block(nc)
            free(g_we)
            for pre in ("we1", "we3", "we2", "y", "yos", "hx", "slb", "hsh",
                        "y2", "xpl", "ob"):
                em.retire_prefix(pre)
        free(g_d)

    return nc



def kernel(**inputs):
    x = np.ascontiguousarray(np.asarray(inputs["L0_attn_encodings"], dtype=np.float32))
    attn_W = np.asarray(inputs["L0_attn_attn_W"], dtype=np.float32)
    attn_b = np.asarray(inputs["L0_attn_attn_b"], dtype=np.float32)
    out_W = np.asarray(inputs["L0_attn_out_W"], dtype=np.float32)
    out_b = np.asarray(inputs["L0_attn_out_b"], dtype=np.float32)
    gate_w = np.asarray(inputs["L0_ffn_gate_w"], dtype=np.float32)
    gate_b = np.asarray(inputs["L0_ffn_gate_b"], dtype=np.float32)
    we1 = np.asarray(inputs["L0_ffn_We1"], dtype=np.float32)
    be1 = np.asarray(inputs["L0_ffn_be1"], dtype=np.float32)
    we2 = np.asarray(inputs["L0_ffn_We2"], dtype=np.float32)
    be2 = np.asarray(inputs["L0_ffn_be2"], dtype=np.float32)
    we3 = np.asarray(inputs["L0_ffn_We3"], dtype=np.float32)
    be3 = np.asarray(inputs["L0_ffn_be3"], dtype=np.float32)
    ws1 = np.asarray(inputs["L0_ffn_ws1"], dtype=np.float32)
    bs1 = np.asarray(inputs["L0_ffn_bs1"], dtype=np.float32)
    ws2 = np.asarray(inputs["L0_ffn_ws2"], dtype=np.float32)
    bs2 = np.asarray(inputs["L0_ffn_bs2"], dtype=np.float32)
    ws3 = np.asarray(inputs["L0_ffn_ws3"], dtype=np.float32)
    bs3 = np.asarray(inputs["L0_ffn_bs3"], dtype=np.float32)

    W3 = attn_W.reshape(D, H, 3, HS)
    wq = np.ascontiguousarray(W3[:, :, 0, :].reshape(D, D))
    wk = np.ascontiguousarray(W3[:, :, 1, :].reshape(D, D))
    wv = np.ascontiguousarray(W3[:, :, 2, :].reshape(D, D))
    b3 = attn_b.reshape(H, 3, HS)
    bq = np.ascontiguousarray(b3[:, 0, :].reshape(D))
    bk = np.ascontiguousarray(b3[:, 1, :].reshape(D))
    bv = np.ascontiguousarray(b3[:, 2, :].reshape(D))

    biascols = np.zeros((128, 160), np.float32)
    for hp in range(HP):
        biascols[:, hp * 2 + 0] = bq[hp * 128:(hp + 1) * 128]
        biascols[:, hp * 2 + 1] = bk[hp * 128:(hp + 1) * 128]
    for e in range(NE):
        for ifc in range(4):
            biascols[:, 16 + e * 4 + ifc] = be1[e, ifc * 128:(ifc + 1) * 128]
            biascols[:, 80 + e * 4 + ifc] = be3[e, ifc * 128:(ifc + 1) * 128]
    for ifc in range(8):
        biascols[:, 144 + ifc] = bs1[ifc * 128:(ifc + 1) * 128]
        biascols[:, 152 + ifc] = bs3[ifc * 128:(ifc + 1) * 128]

    rowvec = np.zeros((1, 3216), np.float32)
    rowvec[0, 0:1024] = out_b
    rowvec[0, 1024:2048] = bs2
    rowvec[0, 2048:2064] = gate_b
    rowvec[0, 2064:3088] = bv
    rowvec[0, 3088:3216] = 1.0

    pp, ff = np.meshgrid(np.arange(128), np.arange(128), indexing="ij")
    masks = np.zeros((128, 512), np.float32)
    for j in range(4):
        masks[:, j * 128:(j + 1) * 128] = np.where(pp <= ff, 0.0, -80000.0)
    eye = np.eye(128, dtype=np.float32)

    shared = {
        "wq": wq, "wk": wk, "wv": wv, "wo": out_W,
        "gwt": np.ascontiguousarray(gate_w.T),
        "we1": np.ascontiguousarray(we1.reshape(NE * D, IN)),
        "we3": np.ascontiguousarray(we3.reshape(NE * D, IN)),
        "we2": np.ascontiguousarray(we2.reshape(NE * IN, D)),
        "ws1": ws1, "ws3": ws3, "ws2": ws2,
        "be2": be2, "biascols": biascols, "rowvec": rowvec,
        "masks": masks, "eye": eye,
    }
    in_maps = [dict(shared, x=np.ascontiguousarray(x[c])) for c in range(8)]

    nc = build()
    res = run_bass_kernel_spmd(nc, in_maps, core_ids=list(range(8)))
    if DBG:
        global _last_results
        _last_results = res.results
    return np.stack([r["out"] for r in res.results], axis=0)
